# version 03
# baseline (speedup 1.0000x reference)
import numpy as np
import ml_dtypes
from contextlib import ExitStack

import concourse.bass as bass
import concourse.mybir as mybir
from concourse.bass_utils import run_bass_kernel_spmd

F32 = mybir.dt.float32
BF16 = mybir.dt.bfloat16
AF = mybir.ActivationFunctionType
ALU = mybir.AluOpType

B, T, HID, H, D = 4, 2048, 2048, 16, 128
NT = T // 128
OWN = 8
HIDDEN = 8192
RSQRT_D = float(128 ** -0.5)
EPS_LN = 1e-5
EPS_ATTN = 1e-5
EPS_RMS = float(np.finfo(np.float32).eps)
N_CORES = 8

ENGS = ["sync", "scalar", "tensor", "vector", "gpsimd"]


def own_tiles_for(core):
    if core % 2 == 0:
        return list(range(4, 12))
    return [0, 1, 2, 3, 12, 13, 14, 15]


def perm_tiles_for(core):
    own = own_tiles_for(core)
    return own + [t for t in range(NT) if t not in own]


def build_masks(core):
    perm = perm_tiles_for(core)
    masks = np.zeros((2 * NT, 128, 512), dtype=np.float32)
    p = np.arange(128)
    for g in range(2):
        for ki in range(NT):
            gk = perm[ki]
            m = masks[g * NT + ki]
            for cb in range(4):
                gq = perm[4 * g + cb]
                if gq > gk:
                    m[:, cb * 128:(cb + 1) * 128] = 1.0
                elif gq == gk:
                    blk = (np.arange(128)[None, :] >= p[:, None])
                    m[:, cb * 128:(cb + 1) * 128] = blk.astype(np.float32)
    return masks.astype(ml_dtypes.bfloat16)


class Sem:
    def __init__(self, handle):
        self.h = handle
        self.n = 0


class MK:

    def __init__(self, nc, ctx):
        self.nc = nc
        e = ctx.enter_context
        self.sems = {
            "tensor": Sem(e(nc.semaphore("semTE"))),
            "scalar": Sem(e(nc.semaphore("semSC"))),
            "vector": Sem(e(nc.semaphore("semVE"))),
            "gpsimd": Sem(e(nc.semaphore("semGP"))),
            "dma_sync": Sem(e(nc.semaphore("semLD"))),
            "dma_gpsimd": Sem(e(nc.semaphore("semGD"))),
            "dma_scalar": Sem(e(nc.semaphore("semSD"))),
        }
        self.ev = {}

    def phase(self, name):
        return Ph(self, name)


PARANOID = [0]
PARANOID_PHASES = set()


class Ph:
    _auto = [0]

    def __init__(self, mk, name):
        self.mk = mk
        self.name = name
        self.items = {e: [] for e in ENGS}
        self.seq = []

    def op(self, eng, fn, ev=None, waits=(), dma=False):
        waits = list(waits)
        par = PARANOID[0] or any(self.name.startswith(p)
                                 for p in PARANOID_PHASES)
        if par:
            if ev is None:
                Ph._auto[0] += 1
                ev = f"__auto{Ph._auto[0]}"
            if self.seq:
                peng, pidx = self.seq[-1]
                pev = self.items[peng][pidx][1]
                if pev is not None:
                    waits.append(pev)
        self.items[eng].append((fn, ev, waits, dma))
        self.seq.append((eng, len(self.items[eng]) - 1))

    def emit(self):
        mk = self.mk
        sim = {k: s.n for k, s in mk.sems.items()}
        prod = {}
        for eng in ENGS:
            for i, (fn, ev, waits, dma) in enumerate(self.items[eng]):
                if dma:
                    key = "dma_" + eng if eng != "sync" else "dma_sync"
                    sim[key] += 16
                    if ev:
                        assert ev not in prod, ev
                        mk.ev[ev] = (key, sim[key])
                        prod[ev] = (eng, i)
                elif ev:
                    sim[eng] += 1
                    assert ev not in prod, ev
                    mk.ev[ev] = (eng, sim[eng])
                    prod[ev] = (eng, i)
        ids = {}
        nodes = []
        for eng in ENGS:
            for i in range(len(self.items[eng])):
                ids[(eng, i)] = len(nodes)
                nodes.append((eng, i))
        adj = [[] for _ in nodes]
        for eng in ENGS:
            for i in range(1, len(self.items[eng])):
                adj[ids[(eng, i - 1)]].append(ids[(eng, i)])
            for i, (fn, ev, waits, dma) in enumerate(self.items[eng]):
                for w in waits:
                    if w in prod:
                        adj[ids[prod[w]]].append(ids[(eng, i)])
                    else:
                        assert w in mk.ev, f"unknown event {w} in {self.name}"
        state = [0] * len(nodes)

        def dfs(u, stack):
            state[u] = 1
            stack.append(u)
            for v in adj[u]:
                if state[v] == 1:
                    cyc = stack[stack.index(v):] + [v]
                    raise RuntimeError(
                        f"cycle in phase {self.name}: "
                        f"{[nodes[x] for x in cyc]}")
                if state[v] == 0:
                    dfs(v, stack)
            stack.pop()
            state[u] = 2

        for u in range(len(nodes)):
            if state[u] == 0:
                dfs(u, [])

        pre = {k: s.n for k, s in mk.sems.items()}
        nc = mk.nc
        with nc.Block(self.name) as blk:
            for eng in ENGS:
                if not self.items[eng]:
                    continue

                def body(engine, eng=eng):
                    own = {eng, "dma_" + eng}
                    for k, v in pre.items():
                        if v > 0 and k not in own:
                            engine.wait_ge(mk.sems[k].h, v)
                    for (fn, ev, waits, dma) in self.items[eng]:
                        for w in waits:
                            k, v = mk.ev[w]
                            engine.wait_ge(mk.sems[k].h, v)
                        if dma:
                            key = "dma_" + eng
                            if mk.sems[key].n > 0:
                                engine.wait_ge(mk.sems[key].h, mk.sems[key].n)
                        instr = fn(engine)
                        if dma:
                            instr.then_inc(mk.sems[key].h, 16)
                            mk.sems[key].n += 16
                        elif ev:
                            instr.then_inc(mk.sems[eng].h, 1)
                            mk.sems[eng].n += 1

                getattr(blk, eng)(body)
        for k, s in mk.sems.items():
            assert s.n == sim[k], (self.name, k, s.n, sim[k])


def build_program(debug=None):
    nc = bass.Bass("TRN2", target_bir_lowering=False)

    xT = nc.dram_tensor("xT", [HID, T], F32, kind="ExternalInput").ap()
    xo = nc.dram_tensor("xo", [OWN * 128, HID], F32, kind="ExternalInput").ap()
    wqT = nc.dram_tensor("wqT", [HID, HID], F32, kind="ExternalInput").ap()
    wkT = nc.dram_tensor("wkT", [HID, HID], F32, kind="ExternalInput").ap()
    wvT = nc.dram_tensor("wvT", [HID, HID], F32, kind="ExternalInput").ap()
    woT = nc.dram_tensor("woT", [HID, HID], F32, kind="ExternalInput").ap()
    wfcT = nc.dram_tensor("wfcT", [HID, HIDDEN], F32, kind="ExternalInput").ap()
    wprT = nc.dram_tensor("wprT", [HIDDEN, HID], F32, kind="ExternalInput").ap()
    gbc = nc.dram_tensor("gbc", [128, 512], F32, kind="ExternalInput").ap()
    brow = nc.dram_tensor("brow", [1, 512], F32, kind="ExternalInput").ap()
    masks = nc.dram_tensor("masks", [2 * NT, 128, 512], BF16,
                           kind="ExternalInput").ap()
    out = nc.dram_tensor("out", [OWN * 128, HID], F32,
                         kind="ExternalOutput").ap()

    w1fc_d = nc.dram_tensor("w1fc_d", [HID, HIDDEN], BF16, kind="Internal").ap()
    w1pr_d = nc.dram_tensor("w1pr_d", [HIDDEN, HID], BF16, kind="Internal").ap()
    x1_d = nc.dram_tensor("x1_d", [OWN * 128, HID], F32, kind="Internal").ap()
    x1T_d = nc.dram_tensor("x1T_d", [HID, OWN * 128], BF16, kind="Internal").ap()

    ident_t = nc.inline_tensor(np.eye(128, dtype=ml_dtypes.bfloat16), "identC")
    onesrow_t = nc.inline_tensor(np.ones((1, 128), dtype=ml_dtypes.bfloat16),
                                 "onesrowC")
    onescol_t = nc.inline_tensor(np.ones((128, 1), dtype=ml_dtypes.bfloat16),
                                 "onescolC")
    onesf_t = nc.inline_tensor(np.ones((1, 1), dtype=np.float32), "onesfC")
    g64_np = np.zeros((16, 128, 32), dtype=ml_dtypes.bfloat16)
    r_np = np.zeros((16, 32, 128), dtype=ml_dtypes.bfloat16)
    for pc in range(16):
        g64_np[pc, :64, 2 * pc] = 1
        g64_np[pc, 64:, 2 * pc + 1] = 1
        r_np[pc, 2 * pc, :64] = 1
        r_np[pc, 2 * pc + 1, 64:] = 1
    g64_t = nc.inline_tensor(g64_np.reshape(16 * 128, 32), "g64C")
    r_t = nc.inline_tensor(r_np.reshape(16 * 32, 128), "rmatC")
    CONSTS = [1.0, 128.0, EPS_LN, 128.0 * EPS_LN, 1.0 / HIDDEN, EPS_RMS]
    cvals_np = np.broadcast_to(np.asarray(CONSTS, np.float32)[None, :],
                               (128, len(CONSTS)))
    cvals_t = nc.inline_tensor(np.ascontiguousarray(cvals_np), "cvalsC")

    dbg = {}

    def dbg_tensor(name, shape, dtype=F32):
        h = nc.dram_tensor(name, shape, dtype, kind="ExternalOutput").ap()
        dbg[name] = h
        return h

    ctx = ExitStack()
    e = ctx.enter_context
    mk = MK(nc, ctx)
    S = mk.sems

    ident = e(nc.sbuf_tensor("identS", [128, 128], BF16))
    onesrow = e(nc.sbuf_tensor("onesrowS", [1, 128], BF16))
    onescol = e(nc.sbuf_tensor("onescolS", [128, 1], BF16))
    onesf = e(nc.sbuf_tensor("onesfS", [1, 1], F32))
    gbc_s = e(nc.sbuf_tensor("gbcS", [128, 512], F32))
    brow_s = e(nc.sbuf_tensor("browS", [1, 512], BF16))
    brow_f = e(nc.sbuf_tensor("browF", [1, 512], F32))
    ic_col = e(nc.sbuf_tensor("iccol", [128, 1], F32))
    tiny = e(nc.sbuf_tensor("tinyS", [128, 4], F32))
    tinyb = e(nc.sbuf_tensor("tinyB", [128, 1], BF16))
    rstd_cols = e(nc.sbuf_tensor("rstdcols", [128, 8], F32))
    stat6 = e(nc.sbuf_tensor("stat6", [128, 24], F32))
    stg = e(nc.sbuf_tensor("stg", [128, 8], F32))
    stat2 = e(nc.sbuf_tensor("stat2", [128, 2], F32))
    cvals = e(nc.sbuf_tensor("cvals", [128, 6], F32))
    sm_a = e(nc.sbuf_tensor("sm_a", [128, 4], F32))
    sm_b = e(nc.sbuf_tensor("sm_b", [128, 4], F32))
    sm_c = e(nc.sbuf_tensor("sm_c", [128, 4], F32))

    ph = mk.phase("init")
    for dst, src in [(ident, ident_t), (onesrow, onesrow_t),
                     (onescol, onescol_t), (onesf, onesf_t)]:
        ph.op("sync", lambda s, d=dst, t=src: s.dma_start(out=d[:], in_=t.ap()),
              dma=True)
    ph.op("sync", lambda s: s.dma_start(out=gbc_s[:], in_=gbc), dma=True)
    ph.op("sync", lambda s: s.dma_start(out=cvals[:], in_=cvals_t.ap()),
          dma=True)
    for ci, cv in enumerate(CONSTS):
        nc.const_aps.register(F32, cv, cvals[:, ci:ci + 1])
        f32v = float(np.float32(cv))
        if f32v != cv:
            nc.const_aps.register(F32, f32v, cvals[:, ci:ci + 1])
    ph.op("sync", lambda s: s.dma_start(out=brow_f[:], in_=brow), ev="ld_brow",
          dma=True)

    ph.op("vector", lambda v: v.memset(tiny[:, 0:1], 0.5 + 2.0 ** -12),
          ev="ic1")
    ph.op("vector", lambda v: v.tensor_copy(out=tinyb[:], in_=tiny[:, 0:1]),
          ev="ic2", waits=["ic1"])
    ph.op("vector", lambda v: v.tensor_copy(out=tiny[:, 1:2], in_=tinyb[:]),
          ev="ic3", waits=["ic2"])
    ph.op("vector", lambda v: v.tensor_scalar_mul(
        out=tiny[:, 2:3], in0=tiny[:, 1:2], scalar1=-4.0),
        ev="ic4", waits=["ic3"])
    ph.op("vector", lambda v: v.tensor_scalar_add(
        out=tiny[:, 2:3], in0=tiny[:, 2:3], scalar1=2.5009765625),
        ev="ic5", waits=["ic4"])
    ph.op("vector", lambda v: v.reciprocal(out=ic_col[:], in_=tiny[:, 2:3]),
        ev="ic6", waits=["ic5"])
    ph.op("vector", lambda v: v.tensor_copy(out=brow_s[:], in_=brow_f[:]),
        ev="initv", waits=["ic6", "ld_brow"])
    ph.emit()

    tern = ExitStack()
    te_ = tern.enter_context
    g64_s = te_(nc.sbuf_tensor("g64S", [128, 16, 32], BF16))
    r_s = te_(nc.sbuf_tensor("rS", [32, 16, 128], BF16))
    t_stage = te_(nc.sbuf_tensor("t_stage", [128, 16, 512], F32))
    t_a = te_(nc.sbuf_tensor("t_a", [128, 16, 512], BF16))
    t_sg = te_(nc.sbuf_tensor("t_sg", [128, 16, 512], BF16))
    t_m = te_(nc.sbuf_tensor("t_m", [128, 16, 512], BF16))
    t_w1 = te_(nc.sbuf_tensor("t_w1", [128, 16, 512], BF16))
    t_sbf = te_(nc.sbuf_tensor("t_sbf", [32, 512], BF16))
    t_sbc = te_(nc.sbuf_tensor("t_sbc", [128, 2, 512], BF16))
    p_tsum = te_(nc.psum_tensor("p_tsum", [32, 512], F32))
    p_sbc = [te_(nc.psum_tensor(f"p_sbc{i}", [128, 512], F32)) for i in (0, 1)]

    ph = mk.phase("ternconst")
    ph.op("sync", lambda s: s.dma_start(
        out=g64_s[:], in_=g64_t.ap().rearrange("(c p) g -> p c g", p=128)),
        dma=True)
    ph.op("sync", lambda s: s.dma_start(
        out=r_s[:], in_=r_t.ap().rearrange("(c p) g -> p c g", p=32)),
        dma=True)
    ph.emit()

    tern_blocks = [(wfcT, w1fc_d, 0, cb) for cb in range(16)]
    tern_blocks += [(wprT, w1pr_d, rb, cb) for rb in range(4) for cb in range(4)]

    for bi, (src, dst, rb, cb) in enumerate(tern_blocks):
        src_v = src.rearrange("(r pc p) f -> r pc p f", p=128, pc=16)
        dst_v = dst.rearrange("(r pc p) f -> r pc p f", p=128, pc=16)
        ph = mk.phase(f"tern{bi}")
        ph.op("sync", lambda s, sv=src_v, rb=rb, cb=cb: s.dma_start(
            out=t_stage[:],
            in_=sv[rb, :, :, cb * 512:(cb + 1) * 512]
            .rearrange("pc p f -> p pc f")), ev="tld", dma=True)

        def f_sg(sc):
            for pc in range(16):
                ins = sc.activation(out=t_sg[:, pc, :], in_=t_stage[:, pc, :],
                                    func=AF.Sign)
            return ins

        ph.op("scalar", f_sg, ev="tsg", waits=["tld"])

        def f_abs(v):
            for pc in range(16):
                ins = v.tensor_tensor(out=t_a[:, pc, :], in0=t_stage[:, pc, :],
                                      in1=t_sg[:, pc, :], op=ALU.mult)
            return ins

        ph.op("vector", f_abs, ev="tabs", waits=["tld", "tsg"])

        def f_gsum(t):
            for pc in range(16):
                ins = t.matmul(p_tsum[:], g64_s[:, pc, :], t_a[:, pc, :],
                               start=(pc == 0), stop=(pc == 15))
            return ins

        ph.op("tensor", f_gsum, ev="tgsum", waits=["tabs"])

        def f_sbf(v):
            v.tensor_scalar_mul(out=t_sbf[:], in0=p_tsum[:], scalar1=1.0 / 64.0)
            return v.tensor_scalar_max(out=t_sbf[:], in0=t_sbf[:], scalar1=1e-8)

        ph.op("vector", f_sbf, ev="tsbf", waits=["tgsum"])

        for pc in range(16):
            w = ["tsbf"]
            if pc >= 2:
                w.append(f"tsbcev{pc - 2}")
            ph.op("tensor",
                  lambda t, pc=pc: t.matmul(p_sbc[pc % 2][:], r_s[:, pc, :],
                                            t_sbf[:], start=True, stop=True),
                  ev=f"tsbc{pc}", waits=w)
            wse = [f"tsbc{pc}"]
            if pc >= 2:
                wse.append(f"tw1_{pc - 2}")
            ph.op("scalar",
                  lambda sc, pc=pc: sc.copy(out=t_sbc[:, pc % 2, :],
                                            in_=p_sbc[pc % 2][:]),
                  ev=f"tsbcev{pc}", waits=wse)

            def f_w1(v, pc=pc):
                sbc = t_sbc[:, pc % 2, :]
                v.scalar_tensor_tensor(out=t_m[:, pc, :], in0=t_a[:, pc, :],
                                       scalar=ic_col[:, 0:1], in1=sbc,
                                       op0=ALU.mult, op1=ALU.is_gt)
                v.tensor_tensor(out=t_w1[:, pc, :], in0=t_sg[:, pc, :],
                                in1=sbc, op=ALU.mult)
                return v.tensor_tensor(out=t_w1[:, pc, :], in0=t_w1[:, pc, :],
                                       in1=t_m[:, pc, :], op=ALU.mult)

            ph.op("vector", f_w1, ev=f"tw1_{pc}",
                  waits=[f"tsbcev{pc}", "tsg"])

        ph.op("gpsimd", lambda g, dv=dst_v, rb=rb, cb=cb: g.dma_start(
            out=dv[rb, :, :, cb * 512:(cb + 1) * 512]
            .rearrange("pc p f -> p pc f"),
            in_=t_w1[:]), ev=f"tw1st{bi}", waits=["tw1_15"], dma=True)
        ph.emit()

    tern.close()

    if debug == "tern":
        d1 = dbg_tensor("dbg_w1fc", [HID, HIDDEN], BF16)
        d2 = dbg_tensor("dbg_w1pr", [HIDDEN, HID], BF16)
        d3 = dbg_tensor("dbg_ta", [128, 16, 512], BF16)
        d4 = dbg_tensor("dbg_tm", [128, 16, 512], BF16)
        d5 = dbg_tensor("dbg_tsbc", [128, 2, 512], BF16)
        d6 = dbg_tensor("dbg_ic", [128, 1], F32)
        d8 = dbg_tensor("dbg_tiny", [128, 4], F32)
        d9 = dbg_tensor("dbg_tinyb", [128, 1], BF16)
        d7 = dbg_tensor("dbg_tsbf", [32, 512], BF16)
        ph = mk.phase("dumptern")
        ph.op("gpsimd", lambda g: g.dma_start(out=d3, in_=t_a[:]), dma=True)
        ph.op("gpsimd", lambda g: g.dma_start(out=d4, in_=t_m[:]), dma=True)
        ph.op("gpsimd", lambda g: g.dma_start(out=d5, in_=t_sbc[:]), dma=True)
        ph.op("gpsimd", lambda g: g.dma_start(out=d6, in_=ic_col[:]), dma=True)
        ph.op("gpsimd", lambda g: g.dma_start(out=d8, in_=tiny[:]), dma=True)
        ph.op("gpsimd", lambda g: g.dma_start(out=d9, in_=tinyb[:]), dma=True)
        ph.op("gpsimd", lambda g: g.dma_start(out=d7, in_=t_sbf[:]), dma=True)
        ph.op("gpsimd", lambda g: g.dma_start(out=d1, in_=w1fc_d), dma=True)
        ph.op("gpsimd", lambda g: g.dma_start(out=d2, in_=w1pr_d), dma=True)
        ph.op("gpsimd", lambda g: g.wait_ge(S["dma_gpsimd"].h,
                                            S["dma_gpsimd"].n) or g.nop())
        ph.emit()
        ctx.close()
        return nc, dbg

    ofm_st = ExitStack()
    o_fm = ofm_st.enter_context(nc.sbuf_tensor("o_fm", [128, 16, 1024], BF16))

    att = ExitStack()
    ae = att.enter_context
    xTbf = ae(nc.sbuf_tensor("xTbf", [128, 16, 2048], BF16))
    kfmT = ae(nc.sbuf_tensor("kfmT", [128, 4, 2048], BF16))
    qfmT = ae(nc.sbuf_tensor("qfmT", [128, 4, 1024], BF16))
    v_sb = ae(nc.sbuf_tensor("v_sb", [128, 16, 4, 132], BF16))
    s2_sb = ae(nc.sbuf_tensor("s2_sb", [128, 16, 512], BF16))
    ktok = ae(nc.sbuf_tensor("ktok", [128, 512], BF16))
    sq_scr = ae(nc.sbuf_tensor("sq_scr", [128, 2, 512], BF16))
    mask_sb = ae(nc.sbuf_tensor("mask_sb", [128, 2, 512], BF16))
    odiv = ae(nc.sbuf_tensor("odiv", [128, 4, 128], BF16))
    zc = ae(nc.sbuf_tensor("zc", [128, 8], F32))

    p_proj = [ae(nc.psum_tensor(f"p_proj{i}", [128, 512], F32)) for i in (0, 1)]
    p_tr = [ae(nc.psum_tensor(f"p_tr{i}", [128, 128], F32)) for i in (0, 1)]
    p_sc = [ae(nc.psum_tensor(f"p_sc{i}", [128, 512], F32)) for i in (0, 1)]
    p_opack = [ae(nc.psum_tensor(f"p_op{i}", [128, 320], F32)) for i in (0, 1)]
    p_o = [p_opack[0][:, 0:132], p_opack[0][:, 160:292],
           p_opack[1][:, 0:132], p_opack[1][:, 160:292]]

    xT_v = xT.rearrange("(dc p) t -> dc p t", p=128)
    ph = mk.phase("loadx")
    for dc in range(16):
        ph.op("gpsimd", lambda g, dc=dc: g.dma_start(out=xTbf[:, dc, :],
                                                     in_=xT_v[dc]),
              ev=f"xl{dc}", dma=True)
    ph.op("vector", lambda v: v.memset(v_sb[:], 1.0))
    ph.emit()

    if debug == "loadx":
        d1 = dbg_tensor("dbg_xTbf", [128, 16, 2048], BF16)
        ph = mk.phase("dumpx")
        ph.op("gpsimd", lambda g: g.dma_start(out=d1, in_=xTbf[:]), dma=True)
        ph.op("gpsimd", lambda g: g.wait_ge(S["dma_gpsimd"].h,
                                            S["dma_gpsimd"].n) or g.nop())
        ph.emit()
        att.close()
        ofm_st.close()
        ctx.close()
        return nc, dbg

    wq_v = wqT.rearrange("(dc p) f -> dc p f", p=128)
    wk_v = wkT.rearrange("(dc p) f -> dc p f", p=128)
    wv_v = wvT.rearrange("(dc p) f -> dc p f", p=128)

    sqs_hist = []
    odiv_hist = []
    otr_hist = []

    def p2a(g, wgrp):
        w_stage, w_bf = wgrp
        ph = mk.phase(f"p2a_{g}")
        specs = [("k", wk_v, True, 16), ("v", wv_v, False, 16),
                 ("q", wq_v, True, 8)]
        trcnt = [0]

        prev_ntt = {0: None, 1: 16, 2: 16}
        for wi, (nm, wv_, hasfm, ntt) in enumerate(specs):
            for qt in range(4):
                w = []
                if (wi, qt) != (0, 0):
                    pq = (wi * 4 + qt) - 1
                    w.append(f"g{g}wc{pq // 4}_{pq % 4}")
                if qt == 0 and wi >= 1:
                    w.append(f"g{g}pj{wi - 1}_{prev_ntt[wi] - 1}")
                ph.op("sync", lambda s, wv_=wv_, qt=qt: s.dma_start(
                    out=w_stage[:],
                    in_=wv_[qt * 4:(qt + 1) * 4, :, g * 512:(g + 1) * 512]
                    .rearrange("dc p f -> p dc f")),
                    ev=f"g{g}wl{wi}_{qt}", waits=w, dma=True)

                def f_cast(v, wi=wi, qt=qt, hasfm=hasfm):
                    for dcl in range(4):
                        dc = qt * 4 + dcl
                        if hasfm:
                            ins = v.tensor_tensor(out=w_bf[:, dc, :],
                                                  in0=w_stage[:, dcl, :],
                                                  in1=gbc_s[:], op=ALU.mult)
                        else:
                            ins = v.tensor_copy(out=w_bf[:, dc, :],
                                                in_=w_stage[:, dcl, :])
                    return ins

                ph.op("vector", f_cast, ev=f"g{g}wc{wi}_{qt}",
                      waits=[f"g{g}wl{wi}_{qt}"])

            for tt in range(ntt):
                w = [f"g{g}wc{wi}_3"]
                if tt >= 2:
                    w.append(f"g{g}pf{wi}_{tt - 2}")
                elif wi >= 1:
                    w.append(f"g{g}pf{wi - 1}_{prev_ntt[wi] - 2 + tt}")

                def f_proj(t, wi=wi, tt=tt, hasfm=hasfm):
                    ps = p_proj[tt % 2]
                    for dc in range(16):
                        ins = t.matmul(
                            ps[:], xTbf[:, dc, tt * 128:(tt + 1) * 128],
                            w_bf[:, dc, :], start=(dc == 0),
                            stop=(dc == 15 and not hasfm))
                    if hasfm:
                        ins = t.matmul(ps[:], onesrow[:], brow_s[:],
                                       start=False, stop=True)
                    return ins

                ph.op("tensor", f_proj, ev=f"g{g}pj{wi}_{tt}", waits=w)

                if nm == "v":
                    def f_vev(sc, tt=tt):
                        ps = p_proj[tt % 2]
                        for seg in range(4):
                            ins = sc.copy(
                                out=v_sb[:, tt, seg, 0:128],
                                in_=ps[:, seg * 128:(seg + 1) * 128])
                        return ins

                    ph.op("scalar", f_vev, ev=f"g{g}pf{wi}_{tt}",
                          waits=[f"g{g}pj{wi}_{tt}"])
                    continue

                def f_bnS(v, wi=wi, tt=tt):
                    ps = p_proj[tt % 2]
                    for seg in range(4):
                        ins = v.bn_stats(out=stat6[:, seg * 6:seg * 6 + 6],
                                         in_=ps[:, seg * 128:(seg + 1) * 128])
                    return ins

                ph.op("vector", f_bnS, ev=f"g{g}bnS{wi}_{tt}",
                      waits=[f"g{g}pj{wi}_{tt}"])

                def f_bn(v, wi=wi, tt=tt):
                    for seg in range(4):
                        ins = v.bn_aggr(
                            out=stg[:, 2 * seg:2 * seg + 2],
                            in_=stat6[:, seg * 6:seg * 6 + 6])
                    return ins

                ph.op("vector", f_bn, ev=f"g{g}bn{wi}_{tt}",
                      waits=[f"g{g}bnS{wi}_{tt}"])

                def f_sq(sc, wi=wi, tt=tt, nm=nm):
                    scl = 128.0 if nm == "q" else 1.0
                    return sc.activation(out=sm_c[:],
                                         in_=stg[:, 1:8:2], func=AF.Sqrt,
                                         scale=scl, bias=scl * EPS_LN)

                ph.op("scalar", f_sq,
                      ev=f"g{g}sq{wi}_{tt}", waits=[f"g{g}bn{wi}_{tt}"])

                ph.op("vector",
                      lambda v: v.reciprocal(out=sm_b[:], in_=sm_c[:]),
                      ev=f"g{g}rc{wi}_{tt}", waits=[f"g{g}sq{wi}_{tt}"])

                def f_fm(v, wi=wi, tt=tt, nm=nm):
                    ps = p_proj[tt % 2]
                    for seg in range(4):
                        ins = v.tensor_scalar(
                            out=ktok[:, seg * 128:(seg + 1) * 128],
                            in0=ps[:, seg * 128:(seg + 1) * 128],
                            scalar1=stg[:, 2 * seg:2 * seg + 1],
                            scalar2=sm_b[:, seg:seg + 1],
                            op0=ALU.subtract, op1=ALU.mult)
                    return ins

                wfm = [f"g{g}rc{wi}_{tt}"]
                if trcnt[0] > 0:
                    wfm.append(f"g{g}tr{trcnt[0] - 1}")
                ph.op("vector", f_fm, ev=f"g{g}pf{wi}_{tt}", waits=wfm)

                for seg in range(4):
                    i = trcnt[0]
                    trcnt[0] += 1
                    w = [f"g{g}pf{wi}_{tt}"]
                    if i >= 2:
                        w.append(f"g{g}trv{i - 2}")
                    ph.op("tensor",
                          lambda t, tt=tt, seg=seg, i=i: t.matmul(
                              p_tr[i % 2][:],
                              ktok[:, seg * 128:(seg + 1) * 128],
                              ident[:], start=True, stop=True),
                          ev=f"g{g}tr{i}", waits=w)

                    def f_trev(v, nm=nm, tt=tt, seg=seg, i=i):
                        if nm == "k":
                            dstT = kfmT[:, seg, tt * 128:(tt + 1) * 128]
                        else:
                            dstT = qfmT[:, seg, tt * 128:(tt + 1) * 128]
                        return v.tensor_copy(out=dstT, in_=p_tr[i % 2][:])

                    ph.op("vector", f_trev, ev=f"g{g}trv{i}",
                          waits=[f"g{g}tr{i}"])
        ph.emit()

    def p2b(g, h):
        hh = 4 * g + h
        ph = mk.phase(f"p2b_{hh}")
        for qg in range(2):
            for ki in range(16):
                mi = len(sqs_hist) % 2
                w = []
                if len(sqs_hist) >= 2:
                    w.append(sqs_hist[-2][1])
                ph.op("sync", lambda s, qg=qg, ki=ki, mi=mi: s.dma_start(
                    out=mask_sb[:, mi, :], in_=masks[qg * NT + ki]),
                    ev=f"h{hh}ml{qg}_{ki}", waits=w, dma=True)
                w = []
                if len(sqs_hist) >= 2:
                    w.append(sqs_hist[-2][0])
                if ki == 0 and odiv_hist:
                    w.append(odiv_hist[-1])
                ph.op("tensor", lambda t, qg=qg, ki=ki, h=h: t.matmul(
                    p_sc[len(sqs_hist) % 2 if False else (qg * 16 + ki) % 2][:],
                    kfmT[:, h, ki * 128:(ki + 1) * 128],
                    qfmT[:, h, qg * 512:(qg + 1) * 512],
                    start=True, stop=True),
                    ev=f"h{hh}sc{qg}_{ki}", waits=w)
                wsq = [f"h{hh}sc{qg}_{ki}"]
                if len(sqs_hist) >= 2:
                    wsq.append(sqs_hist[-2][1])
                ph.op("scalar", lambda sc, qg=qg, ki=ki, mi=mi: sc.activation(
                    out=sq_scr[:, mi, :], in_=p_sc[(qg * 16 + ki) % 2][:],
                    func=AF.Square),
                    ev=f"h{hh}sq{qg}_{ki}", waits=wsq)
                ph.op("vector", lambda v, qg=qg, ki=ki, mi=mi: v.tensor_tensor(
                    out=s2_sb[:, ki, :], in0=sq_scr[:, mi, :],
                    in1=mask_sb[:, mi, :], op=ALU.mult),
                    ev=f"h{hh}s2_{qg}_{ki}",
                    waits=[f"h{hh}sq{qg}_{ki}", f"h{hh}ml{qg}_{ki}"])
                sqs_hist.append((f"h{hh}sq{qg}_{ki}", f"h{hh}s2_{qg}_{ki}"))
            for qtl in range(4):
                w = [f"h{hh}s2_{qg}_15"]
                if qtl == 0 and odiv_hist:
                    w.append(odiv_hist[-1])

                def f_av(t, qg=qg, qtl=qtl, h=h):
                    po = p_o[qtl]
                    for ki in range(16):
                        ins = t.matmul(po[:, 0:129],
                                       s2_sb[:, ki, qtl * 128:(qtl + 1) * 128],
                                       v_sb[:, ki, h, 0:129],
                                       start=(ki == 0), stop=(ki == 15))
                    return ins

                ph.op("tensor", f_av, ev=f"h{hh}av{qg}_{qtl}", waits=w)

            def f_divA(v, qg=qg):
                for qtl in range(4):
                    ins = v.tensor_scalar_add(out=zc[:, qtl:qtl + 1],
                                              in0=p_o[qtl][:, 128:129],
                                              scalar1=EPS_ATTN)
                return ins

            ph.op("vector", f_divA, ev=f"h{hh}da{qg}",
                  waits=[f"h{hh}av{qg}_3"])

            def f_divB(v, qg=qg):
                for qtl in range(4):
                    ins = v.reciprocal(out=zc[:, 4 + qtl:5 + qtl],
                                       in_=zc[:, qtl:qtl + 1])
                return ins

            ph.op("vector", f_divB, ev=f"h{hh}dr{qg}",
                  waits=[f"h{hh}da{qg}"])

            def f_div(v, qg=qg):
                for qtl in range(4):
                    ins = v.tensor_scalar_mul(out=odiv[:, qtl, :],
                                              in0=p_o[qtl][:, 0:128],
                                              scalar1=zc[:, 4 + qtl:5 + qtl])
                return ins

            ph.op("vector", f_div, ev=f"h{hh}od{qg}",
                  waits=[f"h{hh}dr{qg}"])
            odiv_hist.append(f"h{hh}od{qg}")
            for qtl in range(4):
                i = len(otr_hist)
                w = [f"h{hh}od{qg}"]
                if i >= 2:
                    w.append(otr_hist[-2])
                ph.op("tensor", lambda t, qtl=qtl, i=i: t.matmul(
                    p_tr[i % 2][:], odiv[:, qtl, :], ident[:],
                    start=True, stop=True),
                    ev=f"h{hh}otr{qg}_{qtl}", waits=w)
                ph.op("vector", lambda v, hh=hh, qg=qg, qtl=qtl, i=i:
                      v.tensor_copy(
                          out=o_fm[:, hh, (qg * 4 + qtl) * 128:
                                   (qg * 4 + qtl + 1) * 128],
                          in_=p_tr[i % 2][:]),
                      ev=f"h{hh}otrv{qg}_{qtl}", waits=[f"h{hh}otr{qg}_{qtl}"])
                otr_hist.append(f"h{hh}otrv{qg}_{qtl}")
        ph.emit()

    for g in range(4):
        wgrp_st = ExitStack()
        w_stage = wgrp_st.enter_context(
            nc.sbuf_tensor(f"w_stage{g}", [128, 4, 512], F32))
        w_bf = wgrp_st.enter_context(
            nc.sbuf_tensor(f"w_bf{g}", [128, 16, 512], BF16))
        p2a(g, (w_stage, w_bf))
        wgrp_st.close()
        for h in range(4):
            p2b(g, h)

    if debug == "attn":
        d1 = dbg_tensor("dbg_ofm", [128, 16, 1024], BF16)
        d2 = dbg_tensor("dbg_kfmT", [128, 4, 2048], BF16)
        d3 = dbg_tensor("dbg_qfmT", [128, 4, 1024], BF16)
        d4 = dbg_tensor("dbg_vsb", [128, 16, 4, 132], BF16)
        d5 = dbg_tensor("dbg_s2", [128, 16, 512], BF16)
        ph = mk.phase("dumpo")
        ph.op("gpsimd", lambda gg: gg.dma_start(out=d2, in_=kfmT[:]), dma=True)
        ph.op("gpsimd", lambda gg: gg.dma_start(out=d3, in_=qfmT[:]), dma=True)
        ph.op("gpsimd", lambda gg: gg.dma_start(out=d4, in_=v_sb[:]), dma=True)
        ph.op("gpsimd", lambda gg: gg.dma_start(out=d5, in_=s2_sb[:]), dma=True)
        ph.op("gpsimd", lambda gg: gg.dma_start(out=d1, in_=o_fm[:]), dma=True)
        ph.op("gpsimd", lambda gg: gg.wait_ge(S["dma_gpsimd"].h,
                                              S["dma_gpsimd"].n) or gg.nop())
        ph.emit()
        att.close()
        ofm_st.close()
        ctx.close()
        return nc, dbg

    att.close()

    p2c = ExitStack()
    ce = p2c.enter_context
    wo_bf = ce(nc.sbuf_tensor("wo_bf", [128, 16, 2048], BF16))
    xo_sb = [ce(nc.sbuf_tensor(f"xo_sb{i}", [128, 2048], F32)) for i in (0, 1)]
    z1 = ce(nc.sbuf_tensor("z1", [128, 2048], F32))
    x1f = ce(nc.sbuf_tensor("x1f", [128, 2048], F32))
    x1bf = ce(nc.sbuf_tensor("x1bf", [128, 2048], BF16))
    x1ts = ce(nc.sbuf_tensor("x1ts", [128, 16, 128], BF16))
    p_w = [ce(nc.psum_tensor(f"p_w{i}", [128, 512], F32)) for i in range(4)]
    p_t2 = [ce(nc.psum_tensor(f"p_t2{i}", [128, 128], F32)) for i in (0, 1)]

    wo_v = woT.rearrange("(dc p) f -> dc p f", p=128)
    x1T_v = x1T_d.rearrange("(dc p) t -> dc p t", p=128)
    xo_v = xo.rearrange("(tt p) f -> tt p f", p=128)
    x1_v = x1_d.rearrange("(tt p) f -> tt p f", p=128)

    ph = mk.phase("wol")
    for dc in range(16):
        ph.op("gpsimd", lambda g, dc=dc: g.dma_start(out=wo_bf[:, dc, :],
                                                     in_=wo_v[dc]),
              ev=f"wol{dc}", dma=True)
    ph.emit()

    for tt in range(8):
        ph = mk.phase(f"p2c_{tt}")
        ph.op("sync", lambda s, tt=tt: s.dma_start(out=xo_sb[tt % 2][:],
                                                   in_=xo_v[tt]),
              ev=f"xol{tt}", dma=True)
        for dg in range(4):
            w = []
            if tt >= 1:
                w.append(f"z1ev{tt - 1}_{dg}")

            def f_wo(t, tt=tt, dg=dg):
                for fc in range(16):
                    ins = t.matmul(p_w[dg][:], o_fm[:, fc, tt * 128:(tt + 1) * 128],
                                   wo_bf[:, fc, dg * 512:(dg + 1) * 512],
                                   start=(fc == 0), stop=(fc == 15))
                return ins

            ph.op("tensor", f_wo, ev=f"wo{tt}_{dg}", waits=w)

            ph.op("vector", lambda v, tt=tt, dg=dg: v.scalar_tensor_tensor(
                out=z1[:, dg * 512:(dg + 1) * 512], in0=p_w[dg][:],
                scalar=1.0, in1=xo_sb[tt % 2][:, dg * 512:(dg + 1) * 512],
                op0=ALU.mult, op1=ALU.add),
                ev=f"z1ev{tt}_{dg}", waits=[f"wo{tt}_{dg}", f"xol{tt}"])

        def f_ln1s(v, tt=tt):
            for sg in range(4):
                ins = v.bn_stats(out=stat6[:, sg * 6:sg * 6 + 6],
                                 in_=z1[:, sg * 512:(sg + 1) * 512])
            return ins

        ph.op("vector", f_ln1s, ev=f"ln1S{tt}",
              waits=[f"z1ev{tt}_{dg}" for dg in range(4)])
        ph.op("vector", lambda v: v.bn_aggr(out=stat2[:], in_=stat6[:]),
              ev=f"ln1a{tt}", waits=[f"ln1S{tt}"])
        ph.op("scalar", lambda sc, tt=tt: sc.activation(
            out=sm_c[:, 0:1], in_=stat2[:, 1:2], func=AF.Sqrt,
            scale=1.0, bias=EPS_LN),
            ev=f"ln1s{tt}", waits=[f"ln1a{tt}"])

        ph.op("vector",
              lambda v: v.reciprocal(out=sm_b[:, 0:1], in_=sm_c[:, 0:1]),
              ev=f"ln1r{tt}", waits=[f"ln1s{tt}"])

        def f_ln1b(v, tt=tt):
            v.tensor_scalar(out=x1f[:], in0=z1[:],
                            scalar1=stat2[:, 0:1], scalar2=sm_b[:, 0:1],
                            op0=ALU.subtract, op1=ALU.mult)
            return v.tensor_copy(out=x1bf[:], in_=x1f[:])

        ph.op("vector", f_ln1b, ev=f"x1c{tt}", waits=[f"ln1r{tt}"])
        ph.op("gpsimd", lambda g, tt=tt: g.dma_start(out=x1_v[tt],
                                                     in_=x1f[:]),
              ev=f"x1sp{tt}", waits=[f"x1c{tt}"], dma=True)
        for dc in range(16):
            i = tt * 16 + dc
            w = [f"x1c{tt}"]
            if dc >= 2:
                w.append(f"x1trv{tt}_{dc - 2}")
            ph.op("tensor", lambda t, tt=tt, dc=dc: t.matmul(
                p_t2[dc % 2][:], x1bf[:, dc * 128:(dc + 1) * 128], ident[:],
                start=True, stop=True),
                ev=f"x1tr{tt}_{dc}", waits=w)
            ph.op("vector", lambda v, tt=tt, dc=dc: v.tensor_copy(
                out=x1ts[:, dc, :], in_=p_t2[dc % 2][:]),
                ev=f"x1trv{tt}_{dc}", waits=[f"x1tr{tt}_{dc}"])
        ph.op("gpsimd", lambda g, tt=tt: g.dma_start(
            out=x1T_v[:, :, tt * 128:(tt + 1) * 128]
            .rearrange("dc p t -> p dc t"),
            in_=x1ts[:]),
            ev=f"x1Tst{tt}", waits=[f"x1trv{tt}_{dc}" for dc in range(16)],
            dma=True)
        ph.emit()

    p2c.close()
    ofm_st.close()

    if debug == "ln1":
        d1 = dbg_tensor("dbg_x1", [OWN * 128, HID], F32)
        ph = mk.phase("dumpx1")
        ph.op("gpsimd", lambda g: g.dma_start(out=d1, in_=x1_d), dma=True)
        ph.op("gpsimd", lambda g: g.wait_ge(S["dma_gpsimd"].h,
                                            S["dma_gpsimd"].n) or g.nop())
        ph.emit()
        ctx.close()
        return nc, dbg

    p3 = ExitStack()
    fe = p3.enter_context
    h_act = fe(nc.sbuf_tensor("h_act", [128, 64, 512], BF16))
    x1Tsb = fe(nc.sbuf_tensor("x1Tsb", [128, 16, 512], BF16))
    wblk = [fe(nc.sbuf_tensor(f"wblk{i}", [128, 16, 512], BF16))
            for i in (0, 1)]
    hr_scr = fe(nc.sbuf_tensor("hr_scr", [128, 2, 512], F32))
    hsq_scr = [fe(nc.sbuf_tensor(f"hsq{i}", [128, 512], BF16)) for i in (0, 1)]
    x1c_sb = [fe(nc.sbuf_tensor(f"x1cs{i}", [128, 512], F32)) for i in (0, 1)]
    z2 = fe(nc.sbuf_tensor("z2", [128, 4, 2048], F32))
    outt = fe(nc.sbuf_tensor("outt", [128, 2048], F32))
    ms_row = fe(nc.sbuf_tensor("ms_row", [1, 512], F32))
    p_h = [fe(nc.psum_tensor(f"p_h{i}", [128, 512], F32)) for i in (0, 1)]
    p_ms = fe(nc.psum_tensor("p_ms", [1, 512], F32))
    p_mini = fe(nc.psum_tensor("p_mini", [128, 4], F32))
    p_y = [fe(nc.psum_tensor(f"p_y{i}", [128, 512], F32)) for i in range(4)]

    w1fc_v = w1fc_d.rearrange("(dc p) f -> dc p f", p=128)
    w1pr_v = w1pr_d.rearrange("(rb hc p) f -> rb hc p f", p=128, hc=16)
    out_v = out.rearrange("(tt p) f -> tt p f", p=128)
    x1_v3 = x1_d.rearrange("(tt p) f -> tt p f", p=128)

    for tg in range(2):
        ph = mk.phase(f"p3a_{tg}")
        ph.op("sync", lambda s, tg=tg: s.dma_start(
            out=x1Tsb[:],
            in_=x1T_v[:, :, tg * 512:(tg + 1) * 512]
            .rearrange("dc p t -> p dc t")),
            ev=f"A{tg}x1T", dma=True)
        for blkq in range(16):
            w = [f"A{tg}re{blkq - 1}_3"] if blkq >= 1 else []
            ph.op("sync", lambda s, blkq=blkq: s.dma_start(
                out=wblk[blkq % 2][:],
                in_=w1fc_v[:, :, blkq * 512:(blkq + 1) * 512]
                .rearrange("dc p f -> p dc f")),
                ev=f"A{tg}wl{blkq}", waits=w, dma=True)
            for htl in range(4):
                ht = blkq * 4 + htl
                w = [f"A{tg}wl{blkq}"]
                if ht >= 2:
                    pht = ht - 2
                    w.append(f"A{tg}re{pht // 4}_{pht % 4}")

                def f_h(t, blkq=blkq, htl=htl, ht=ht, tg=tg):
                    ps = p_h[ht % 2]
                    for dc in range(16):
                        ins = t.matmul(
                            ps[:], wblk[blkq % 2][:, dc,
                                                  htl * 128:(htl + 1) * 128],
                            x1Tsb[:, dc, :],
                            start=(dc == 0), stop=(dc == 15))
                    return ins

                w.append(f"A{tg}x1T")
                ph.op("tensor", f_h, ev=f"A{tg}hp{blkq}_{htl}", waits=w)
                w = [f"A{tg}hp{blkq}_{htl}"]
                if ht >= 2:
                    pht = ht - 2
                    w.append(f"A{tg}ha{pht // 4}_{pht % 4}")
                ph.op("scalar", lambda sc, ht=ht: sc.activation(
                    out=hr_scr[:, ht % 2, :], in_=p_h[ht % 2][:],
                    func=AF.Relu),
                    ev=f"A{tg}re{blkq}_{htl}", waits=w)

                def f_ha(v, ht=ht, blkq=blkq, htl=htl):
                    hr = hr_scr[:, ht % 2, :]
                    v.tensor_tensor(out=h_act[:, ht, :], in0=hr,
                                    in1=hr, op=ALU.mult)
                    return v.tensor_tensor(out=hsq_scr[ht % 2][:],
                                           in0=h_act[:, ht, :],
                                           in1=h_act[:, ht, :], op=ALU.mult)

                w = [f"A{tg}re{blkq}_{htl}"]
                if ht >= 2:
                    w.append(f"A{tg}zm{ht - 2}")
                ph.op("vector", f_ha, ev=f"A{tg}ha{blkq}_{htl}", waits=w)
                ph.op("tensor", lambda t, ht=ht: t.matmul(
                    p_ms[:], onescol[:], hsq_scr[ht % 2][:],
                    start=(ht == 0), stop=(ht == 63)),
                    ev=f"A{tg}zm{ht}", waits=[f"A{tg}ha{blkq}_{htl}"])
        ph.op("vector", lambda v: v.tensor_copy(out=ms_row[:], in_=p_ms[:]),
              ev=f"A{tg}msr", waits=[f"A{tg}zm63"])
        for j in range(4):
            ph.op("tensor", lambda t, j=j: t.matmul(
                p_mini[:, j:j + 1], ms_row[0:1, j * 128:(j + 1) * 128],
                onesf[:], start=True, stop=True),
                ev=f"A{tg}mt{j}", waits=[f"A{tg}msr"])

        ph.op("scalar", lambda sc: sc.activation(
            out=sm_c[:], in_=p_mini[:], func=AF.Sqrt,
            scale=1.0 / HIDDEN, bias=EPS_RMS),
            ev=f"A{tg}rms", waits=[f"A{tg}mt{j}" for j in range(4)])

        ph.op("vector", lambda v, tg=tg: v.reciprocal(
            out=rstd_cols[:, tg * 4:(tg + 1) * 4], in_=sm_c[:]),
            ev=f"A{tg}rmb", waits=[f"A{tg}rms"])
        ph.emit()

        ph = mk.phase(f"p3b_{tg}")
        nload = [0]
        for dq in range(4):
            for hmc in range(4):
                w = []
                if dq * 4 + hmc >= 2:
                    pi = dq * 4 + hmc - 2
                    w.append(f"B{tg}ym{pi // 4}_{pi % 4}")
                ph.op("sync", lambda s, dq=dq, hmc=hmc: s.dma_start(
                    out=wblk[hmc % 2][:],
                    in_=w1pr_v[hmc, :, :, dq * 512:(dq + 1) * 512]
                    .rearrange("hc p f -> p hc f")),
                    ev=f"B{tg}wl{dq}_{hmc}", waits=w, dma=True)
                w = [f"B{tg}wl{dq}_{hmc}"]
                if hmc == 0 and dq >= 1:
                    w += [f"B{tg}z2_{dq - 1}_{ttl}" for ttl in range(4)]

                def f_y(t, dq=dq, hmc=hmc, tg=tg):
                    for hcl in range(16):
                        hc = hmc * 16 + hcl
                        for ttl in range(4):
                            ins = t.matmul(
                                p_y[ttl][:],
                                h_act[:, hc, ttl * 128:(ttl + 1) * 128],
                                wblk[hmc % 2][:, hcl, :],
                                start=(hc == 0), stop=(hc == 63))
                    return ins

                ph.op("tensor", f_y, ev=f"B{tg}ym{dq}_{hmc}", waits=w)
            for ttl in range(4):
                li = nload[0]
                nload[0] += 1
                w = []
                if li >= 2:
                    lj = li - 2
                    w.append(f"B{tg}z2_{lj // 4}_{lj % 4}")
                ph.op("sync", lambda s, dq=dq, ttl=ttl, li=li, tg=tg:
                      s.dma_start(out=x1c_sb[li % 2][:],
                                  in_=x1_v3[tg * 4 + ttl, :,
                                            dq * 512:(dq + 1) * 512]),
                      ev=f"B{tg}xl{dq}_{ttl}", waits=w, dma=True)
                ph.op("vector", lambda v, dq=dq, ttl=ttl, li=li, tg=tg:
                      v.scalar_tensor_tensor(
                          out=z2[:, ttl, dq * 512:(dq + 1) * 512],
                          in0=p_y[ttl][:],
                          scalar=rstd_cols[:, tg * 4 + ttl:tg * 4 + ttl + 1],
                          in1=x1c_sb[li % 2][:],
                          op0=ALU.mult, op1=ALU.add),
                      ev=f"B{tg}z2_{dq}_{ttl}",
                      waits=[f"B{tg}ym{dq}_3", f"B{tg}xl{dq}_{ttl}"])
        for ttl in range(4):
            def f_ln2s(v, ttl=ttl):
                for sg in range(4):
                    ins = v.bn_stats(out=stat6[:, sg * 6:sg * 6 + 6],
                                     in_=z2[:, ttl, sg * 512:(sg + 1) * 512])
                return ins

            ph.op("vector", f_ln2s, ev=f"B{tg}lnS{ttl}",
                  waits=[f"B{tg}z2_3_{ttl}"])
            ph.op("vector", lambda v: v.bn_aggr(out=stat2[:], in_=stat6[:]),
                  ev=f"B{tg}lna{ttl}", waits=[f"B{tg}lnS{ttl}"])
            ph.op("scalar", lambda sc: sc.activation(
                out=sm_c[:, 0:1], in_=stat2[:, 1:2], func=AF.Sqrt,
                scale=1.0, bias=EPS_LN),
                ev=f"B{tg}lns{ttl}", waits=[f"B{tg}lna{ttl}"])

            ph.op("vector",
                  lambda v: v.reciprocal(out=sm_b[:, 0:1], in_=sm_c[:, 0:1]),
                  ev=f"B{tg}lnr{ttl}", waits=[f"B{tg}lns{ttl}"])

            def f_ln2b(v, ttl=ttl):
                return v.tensor_scalar(out=outt[:], in0=z2[:, ttl, :],
                                       scalar1=stat2[:, 0:1],
                                       scalar2=sm_b[:, 0:1],
                                       op0=ALU.subtract, op1=ALU.mult)

            wl2 = [f"B{tg}lnr{ttl}"]
            if ttl >= 1:
                wl2.append(f"B{tg}st{ttl - 1}")
            ph.op("vector", f_ln2b, ev=f"B{tg}lnb{ttl}", waits=wl2)
            ph.op("gpsimd", lambda g, ttl=ttl, tg=tg: g.dma_start(
                out=out_v[tg * 4 + ttl], in_=outt[:]),
                ev=f"B{tg}st{ttl}", waits=[f"B{tg}lnb{ttl}"], dma=True)
            if ttl < 3:
                pass
        ph.emit()

    if debug == "ffn":
        d1 = dbg_tensor("dbg_z2", [128, 4, 2048], F32)
        d2 = dbg_tensor("dbg_hact", [128, 64, 512], BF16)
        d3 = dbg_tensor("dbg_rstd", [128, 8], F32)
        ph = mk.phase("dumpffn")
        ph.op("gpsimd", lambda g: g.dma_start(out=d1, in_=z2[:]), dma=True)
        ph.op("gpsimd", lambda g: g.dma_start(out=d2, in_=h_act[:]), dma=True)
        ph.op("gpsimd", lambda g: g.dma_start(out=d3, in_=rstd_cols[:]),
              dma=True)
        ph.emit()

    p3.close()

    ph = mk.phase("fin")
    ph.op("gpsimd", lambda g: g.wait_ge(S["dma_gpsimd"].h, S["dma_gpsimd"].n))
    ph.emit()

    ctx.close()
    return nc, dbg


def _stage_inputs(inputs):
    x = np.asarray(inputs["L0_attn_hidden_states"])
    Wq = np.asarray(inputs["L0_attn_Wq"])
    Wk = np.asarray(inputs["L0_attn_Wk"])
    Wv = np.asarray(inputs["L0_attn_Wv"])
    Wo = np.asarray(inputs["L0_attn_Wo"])
    gamma = np.asarray(inputs["L0_attn_gamma"])
    beta = np.asarray(inputs["L0_attn_beta"])
    Wfc = np.asarray(inputs["L0_ffn_W_fc"])
    Wpr = np.asarray(inputs["L0_ffn_W_proj"])

    wqT = np.ascontiguousarray(Wq.T)
    wkT = np.ascontiguousarray(Wk.T)
    wvT = np.ascontiguousarray(Wv.T)
    woT = np.ascontiguousarray(Wo.T)
    wfcT = np.ascontiguousarray(Wfc.T)
    wprT = np.ascontiguousarray(Wpr.T)
    gbc = np.ascontiguousarray(
        np.broadcast_to(np.tile(gamma, 4)[None, :], (128, 512))).astype(
            np.float32, copy=False)
    brow = np.ascontiguousarray(np.tile(beta, 4)[None, :]).astype(
        np.float32, copy=False)

    in_maps = []
    perms = []
    for c in range(N_CORES):
        b = c // 2
        perm = perm_tiles_for(c)
        perms.append(perm)
        cols = np.concatenate([np.arange(t * 128, (t + 1) * 128)
                               for t in perm])
        own_rows = cols[:OWN * 128]
        xT_c = np.ascontiguousarray(x[b].T[:, cols])
        xo_c = np.ascontiguousarray(x[b][own_rows])
        in_maps.append({
            "xT": xT_c, "xo": xo_c,
            "wqT": wqT, "wkT": wkT, "wvT": wvT, "woT": woT,
            "wfcT": wfcT, "wprT": wprT,
            "gbc": gbc, "brow": brow,
            "masks": np.ascontiguousarray(build_masks(c)),
        })
    return in_maps, perms


_CACHED = {}


def kernel(**inputs):
    if "nc" not in _CACHED:
        nc, _ = build_program()
        _CACHED["nc"] = nc
    nc = _CACHED["nc"]
    in_maps, perms = _stage_inputs(inputs)
    res = run_bass_kernel_spmd(nc, in_maps, core_ids=list(range(N_CORES)))
    outp = np.zeros((B, T, HID), dtype=np.float32)
    for c in range(N_CORES):
        b = c // 2
        own = perms[c][:OWN]
        r = np.asarray(res.results[c]["out"])
        for i, t in enumerate(own):
            outp[b, t * 128:(t + 1) * 128] = r[i * 128:(i + 1) * 128]
    return outp


# version 04 (final)
# speedup vs baseline: 1.1184x; 1.1184x over previous
import numpy as np
import ml_dtypes
from contextlib import ExitStack

import concourse.bass as bass
import concourse.mybir as mybir
from concourse.bass_utils import run_bass_kernel_spmd

F32 = mybir.dt.float32
BF16 = mybir.dt.bfloat16
AF = mybir.ActivationFunctionType
ALU = mybir.AluOpType

B, T, HID, H, D = 4, 2048, 2048, 16, 128
NT = T // 128
OWN = 8
HIDDEN = 8192
RSQRT_D = float(128 ** -0.5)
EPS_LN = 1e-5
EPS_ATTN = 1e-5
EPS_RMS = float(np.finfo(np.float32).eps)
N_CORES = 8

ENGS = ["sync", "scalar", "tensor", "vector", "gpsimd"]


def own_tiles_for(core):
    if core % 2 == 0:
        return list(range(4, 12))
    return [0, 1, 2, 3, 12, 13, 14, 15]


def perm_tiles_for(core):
    own = own_tiles_for(core)
    return own + [t for t in range(NT) if t not in own]


def build_masks(core):
    perm = perm_tiles_for(core)
    masks = np.zeros((2 * NT, 128, 512), dtype=np.float32)
    p = np.arange(128)
    for g in range(2):
        for ki in range(NT):
            gk = perm[ki]
            m = masks[g * NT + ki]
            for cb in range(4):
                gq = perm[4 * g + cb]
                if gq > gk:
                    m[:, cb * 128:(cb + 1) * 128] = 1.0
                elif gq == gk:
                    blk = (np.arange(128)[None, :] >= p[:, None])
                    m[:, cb * 128:(cb + 1) * 128] = blk.astype(np.float32)
    return masks.astype(ml_dtypes.bfloat16)


class Sem:
    def __init__(self, handle):
        self.h = handle
        self.n = 0


class MK:

    def __init__(self, nc, ctx):
        self.nc = nc
        e = ctx.enter_context
        self.sems = {
            "tensor": Sem(e(nc.semaphore("semTE"))),
            "scalar": Sem(e(nc.semaphore("semSC"))),
            "vector": Sem(e(nc.semaphore("semVE"))),
            "gpsimd": Sem(e(nc.semaphore("semGP"))),
            "dma_sync": Sem(e(nc.semaphore("semLD"))),
            "dma_gpsimd": Sem(e(nc.semaphore("semGD"))),
            "dma_scalar": Sem(e(nc.semaphore("semSD"))),
        }
        self.ev = {}

    def phase(self, name):
        return Ph(self, name)


PARANOID = [0]
PARANOID_PHASES = set()


class Ph:
    _auto = [0]

    def __init__(self, mk, name):
        self.mk = mk
        self.name = name
        self.items = {e: [] for e in ENGS}
        self.seq = []

    def op(self, eng, fn, ev=None, waits=(), dma=False):
        waits = list(waits)
        par = PARANOID[0] or any(self.name.startswith(p)
                                 for p in PARANOID_PHASES)
        if par:
            if ev is None:
                Ph._auto[0] += 1
                ev = f"__auto{Ph._auto[0]}"
            if self.seq:
                peng, pidx = self.seq[-1]
                pev = self.items[peng][pidx][1]
                if pev is not None:
                    waits.append(pev)
        self.items[eng].append((fn, ev, waits, dma))
        self.seq.append((eng, len(self.items[eng]) - 1))

    def emit(self):
        mk = self.mk
        sim = {k: s.n for k, s in mk.sems.items()}
        prod = {}
        for eng in ENGS:
            for i, (fn, ev, waits, dma) in enumerate(self.items[eng]):
                if dma:
                    key = "dma_" + eng if eng != "sync" else "dma_sync"
                    sim[key] += 16
                    if ev:
                        assert ev not in prod, ev
                        mk.ev[ev] = (key, sim[key])
                        prod[ev] = (eng, i)
                elif ev:
                    sim[eng] += 1
                    assert ev not in prod, ev
                    mk.ev[ev] = (eng, sim[eng])
                    prod[ev] = (eng, i)
        ids = {}
        nodes = []
        for eng in ENGS:
            for i in range(len(self.items[eng])):
                ids[(eng, i)] = len(nodes)
                nodes.append((eng, i))
        adj = [[] for _ in nodes]
        for eng in ENGS:
            for i in range(1, len(self.items[eng])):
                adj[ids[(eng, i - 1)]].append(ids[(eng, i)])
            for i, (fn, ev, waits, dma) in enumerate(self.items[eng]):
                for w in waits:
                    if w in prod:
                        adj[ids[prod[w]]].append(ids[(eng, i)])
                    else:
                        assert w in mk.ev, f"unknown event {w} in {self.name}"
        state = [0] * len(nodes)

        def dfs(u, stack):
            state[u] = 1
            stack.append(u)
            for v in adj[u]:
                if state[v] == 1:
                    cyc = stack[stack.index(v):] + [v]
                    raise RuntimeError(
                        f"cycle in phase {self.name}: "
                        f"{[nodes[x] for x in cyc]}")
                if state[v] == 0:
                    dfs(v, stack)
            stack.pop()
            state[u] = 2

        for u in range(len(nodes)):
            if state[u] == 0:
                dfs(u, [])

        pre = {k: s.n for k, s in mk.sems.items()}
        nc = mk.nc
        with nc.Block(self.name) as blk:
            for eng in ENGS:
                if not self.items[eng]:
                    continue

                def body(engine, eng=eng):
                    own = {eng, "dma_" + eng}
                    for k, v in pre.items():
                        if v > 0 and k not in own:
                            engine.wait_ge(mk.sems[k].h, v)
                    for (fn, ev, waits, dma) in self.items[eng]:
                        for w in waits:
                            k, v = mk.ev[w]
                            engine.wait_ge(mk.sems[k].h, v)
                        if dma:
                            key = "dma_" + eng
                            if mk.sems[key].n > 0:
                                engine.wait_ge(mk.sems[key].h, mk.sems[key].n)
                        instr = fn(engine)
                        if dma:
                            instr.then_inc(mk.sems[key].h, 16)
                            mk.sems[key].n += 16
                        elif ev:
                            instr.then_inc(mk.sems[eng].h, 1)
                            mk.sems[eng].n += 1

                getattr(blk, eng)(body)
        for k, s in mk.sems.items():
            assert s.n == sim[k], (self.name, k, s.n, sim[k])


def build_program(debug=None):
    nc = bass.Bass("TRN2", target_bir_lowering=False)

    xT = nc.dram_tensor("xT", [HID, T], F32, kind="ExternalInput").ap()
    xo = nc.dram_tensor("xo", [OWN * 128, HID], F32, kind="ExternalInput").ap()
    wqT = nc.dram_tensor("wqT", [HID, HID], F32, kind="ExternalInput").ap()
    wkT = nc.dram_tensor("wkT", [HID, HID], F32, kind="ExternalInput").ap()
    wvT = nc.dram_tensor("wvT", [HID, HID], F32, kind="ExternalInput").ap()
    woT = nc.dram_tensor("woT", [HID, HID], F32, kind="ExternalInput").ap()
    wfcT = nc.dram_tensor("wfcT", [HID, HIDDEN], F32, kind="ExternalInput").ap()
    wprT = nc.dram_tensor("wprT", [HIDDEN, HID], F32, kind="ExternalInput").ap()
    gbc = nc.dram_tensor("gbc", [128, 512], F32, kind="ExternalInput").ap()
    brow = nc.dram_tensor("brow", [1, 512], F32, kind="ExternalInput").ap()
    masks = nc.dram_tensor("masks", [2 * NT, 128, 512], BF16,
                           kind="ExternalInput").ap()
    out = nc.dram_tensor("out", [OWN * 128, HID], F32,
                         kind="ExternalOutput").ap()

    w1fc_d = nc.dram_tensor("w1fc_d", [HID, HIDDEN], BF16, kind="Internal").ap()
    w1pr_d = nc.dram_tensor("w1pr_d", [HIDDEN, HID], BF16, kind="Internal").ap()
    x1_d = nc.dram_tensor("x1_d", [OWN * 128, HID], F32, kind="Internal").ap()
    x1T_d = nc.dram_tensor("x1T_d", [HID, OWN * 128], BF16, kind="Internal").ap()

    ident_t = nc.inline_tensor(np.eye(128, dtype=ml_dtypes.bfloat16), "identC")
    onesrow_t = nc.inline_tensor(np.ones((1, 128), dtype=ml_dtypes.bfloat16),
                                 "onesrowC")
    onescol_t = nc.inline_tensor(np.ones((128, 1), dtype=ml_dtypes.bfloat16),
                                 "onescolC")
    onesf_t = nc.inline_tensor(np.ones((1, 1), dtype=np.float32), "onesfC")
    g64_np = np.zeros((16, 128, 32), dtype=ml_dtypes.bfloat16)
    r_np = np.zeros((16, 32, 128), dtype=ml_dtypes.bfloat16)
    for pc in range(16):
        g64_np[pc, :64, 2 * pc] = 1
        g64_np[pc, 64:, 2 * pc + 1] = 1
        r_np[pc, 2 * pc, :64] = 1
        r_np[pc, 2 * pc + 1, 64:] = 1
    g64_t = nc.inline_tensor(g64_np.reshape(16 * 128, 32), "g64C")
    r_t = nc.inline_tensor(r_np.reshape(16 * 32, 128), "rmatC")
    CONSTS = [1.0, 128.0, EPS_LN, 128.0 * EPS_LN, 1.0 / HIDDEN, EPS_RMS]
    cvals_np = np.broadcast_to(np.asarray(CONSTS, np.float32)[None, :],
                               (128, len(CONSTS)))
    cvals_t = nc.inline_tensor(np.ascontiguousarray(cvals_np), "cvalsC")

    dbg = {}

    def dbg_tensor(name, shape, dtype=F32):
        h = nc.dram_tensor(name, shape, dtype, kind="ExternalOutput").ap()
        dbg[name] = h
        return h

    ctx = ExitStack()
    e = ctx.enter_context
    mk = MK(nc, ctx)
    S = mk.sems

    ident = e(nc.sbuf_tensor("identS", [128, 128], BF16))
    onesrow = e(nc.sbuf_tensor("onesrowS", [1, 128], BF16))
    onescol = e(nc.sbuf_tensor("onescolS", [128, 1], BF16))
    onesf = e(nc.sbuf_tensor("onesfS", [1, 1], F32))
    gbc_s = e(nc.sbuf_tensor("gbcS", [128, 512], F32))
    brow_s = e(nc.sbuf_tensor("browS", [1, 512], BF16))
    brow_f = e(nc.sbuf_tensor("browF", [1, 512], F32))
    ic_col = e(nc.sbuf_tensor("iccol", [128, 1], F32))
    tiny = e(nc.sbuf_tensor("tinyS", [128, 4], F32))
    tinyb = e(nc.sbuf_tensor("tinyB", [128, 1], BF16))
    rstd_cols = e(nc.sbuf_tensor("rstdcols", [128, 8], F32))
    stat6 = e(nc.sbuf_tensor("stat6", [128, 24], F32))
    stg = e(nc.sbuf_tensor("stg", [128, 8], F32))
    stat2 = e(nc.sbuf_tensor("stat2", [128, 2], F32))
    cvals = e(nc.sbuf_tensor("cvals", [128, 6], F32))
    sm_a = e(nc.sbuf_tensor("sm_a", [128, 4], F32))
    sm_b = e(nc.sbuf_tensor("sm_b", [128, 4], F32))
    sm_c = e(nc.sbuf_tensor("sm_c", [128, 4], F32))

    ph = mk.phase("init")
    for dst, src in [(ident, ident_t), (onesrow, onesrow_t),
                     (onescol, onescol_t), (onesf, onesf_t)]:
        ph.op("sync", lambda s, d=dst, t=src: s.dma_start(out=d[:], in_=t.ap()),
              dma=True)
    ph.op("sync", lambda s: s.dma_start(out=gbc_s[:], in_=gbc), dma=True)
    ph.op("sync", lambda s: s.dma_start(out=cvals[:], in_=cvals_t.ap()),
          dma=True)
    for ci, cv in enumerate(CONSTS):
        nc.const_aps.register(F32, cv, cvals[:, ci:ci + 1])
        f32v = float(np.float32(cv))
        if f32v != cv:
            nc.const_aps.register(F32, f32v, cvals[:, ci:ci + 1])
    ph.op("sync", lambda s: s.dma_start(out=brow_f[:], in_=brow), ev="ld_brow",
          dma=True)

    ph.op("vector", lambda v: v.memset(tiny[:, 0:1], 0.5 + 2.0 ** -12),
          ev="ic1")
    ph.op("vector", lambda v: v.tensor_copy(out=tinyb[:], in_=tiny[:, 0:1]),
          ev="ic2", waits=["ic1"])
    ph.op("vector", lambda v: v.tensor_copy(out=tiny[:, 1:2], in_=tinyb[:]),
          ev="ic3", waits=["ic2"])
    ph.op("vector", lambda v: v.tensor_scalar_mul(
        out=tiny[:, 2:3], in0=tiny[:, 1:2], scalar1=-4.0),
        ev="ic4", waits=["ic3"])
    ph.op("vector", lambda v: v.tensor_scalar_add(
        out=tiny[:, 2:3], in0=tiny[:, 2:3], scalar1=2.5009765625),
        ev="ic5", waits=["ic4"])
    ph.op("vector", lambda v: v.reciprocal(out=ic_col[:], in_=tiny[:, 2:3]),
        ev="ic6", waits=["ic5"])
    ph.op("vector", lambda v: v.tensor_copy(out=brow_s[:], in_=brow_f[:]),
        ev="initv", waits=["ic6", "ld_brow"])
    ph.emit()

    tern = ExitStack()
    te_ = tern.enter_context
    g64_s = te_(nc.sbuf_tensor("g64S", [128, 16, 32], BF16))
    r_s = te_(nc.sbuf_tensor("rS", [32, 16, 128], BF16))
    t_stage = [te_(nc.sbuf_tensor(f"t_stage{i}", [128, 16, 512], F32))
               for i in (0, 1)]
    t_a = te_(nc.sbuf_tensor("t_a", [128, 16, 512], BF16))
    t_sg = te_(nc.sbuf_tensor("t_sg", [128, 16, 512], BF16))
    t_m = te_(nc.sbuf_tensor("t_m", [128, 16, 512], BF16))
    t_w1 = [te_(nc.sbuf_tensor(f"t_w1{i}", [128, 16, 512], BF16))
            for i in (0, 1)]
    t_sbf = te_(nc.sbuf_tensor("t_sbf", [32, 512], BF16))
    t_sbc = te_(nc.sbuf_tensor("t_sbc", [128, 2, 512], BF16))
    p_tsum = te_(nc.psum_tensor("p_tsum", [32, 512], F32))
    p_sbc = [te_(nc.psum_tensor(f"p_sbc{i}", [128, 512], F32)) for i in (0, 1)]

    tern_blocks = [(wfcT, w1fc_d, 0, cb) for cb in range(16)]
    tern_blocks += [(wprT, w1pr_d, rb, cb) for rb in range(4) for cb in range(4)]

    def tern_load(s, bi):
        tsrc, tdst, rb, cb = tern_blocks[bi]
        sv = tsrc.rearrange("(r pc p) f -> r pc p f", p=128, pc=16)
        return s.dma_start(
            out=t_stage[bi % 2][:],
            in_=sv[rb, :, :, cb * 512:(cb + 1) * 512]
            .rearrange("pc p f -> p pc f"))

    def tern_store(g, bi):
        tsrc, tdst, rb, cb = tern_blocks[bi]
        dv = tdst.rearrange("(r pc p) f -> r pc p f", p=128, pc=16)
        return g.dma_start(
            out=dv[rb, :, :, cb * 512:(cb + 1) * 512]
            .rearrange("pc p f -> p pc f"),
            in_=t_w1[bi % 2][:])

    ph = mk.phase("ternconst")
    ph.op("sync", lambda s: s.dma_start(
        out=g64_s[:], in_=g64_t.ap().rearrange("(c p) g -> p c g", p=128)),
        dma=True)
    ph.op("sync", lambda s: s.dma_start(
        out=r_s[:], in_=r_t.ap().rearrange("(c p) g -> p c g", p=32)),
        dma=True)
    ph.op("sync", lambda s: tern_load(s, 0), dma=True)
    ph.emit()

    for bi in range(len(tern_blocks)):
        ph = mk.phase(f"tern{bi}")
        if bi + 1 < len(tern_blocks):
            ph.op("sync", lambda s, bi=bi: tern_load(s, bi + 1), dma=True)
        if bi >= 1:
            ph.op("gpsimd", lambda g, bi=bi: tern_store(g, bi - 1), dma=True)

        def f_sg(sc, bi=bi):
            for pc in range(16):
                ins = sc.activation(out=t_sg[:, pc, :],
                                    in_=t_stage[bi % 2][:, pc, :],
                                    func=AF.Sign)
            return ins

        ph.op("scalar", f_sg, ev="tsg")

        def f_abs(v, bi=bi):
            for pc in range(16):
                ins = v.tensor_tensor(out=t_a[:, pc, :],
                                      in0=t_stage[bi % 2][:, pc, :],
                                      in1=t_sg[:, pc, :], op=ALU.mult)
            return ins

        ph.op("vector", f_abs, ev="tabs", waits=["tsg"])

        def f_gsum(t):
            for pc in range(16):
                ins = t.matmul(p_tsum[:], g64_s[:, pc, :], t_a[:, pc, :],
                               start=(pc == 0), stop=(pc == 15))
            return ins

        ph.op("tensor", f_gsum, ev="tgsum", waits=["tabs"])

        def f_sbf(v):
            v.tensor_scalar_mul(out=t_sbf[:], in0=p_tsum[:], scalar1=1.0 / 64.0)
            return v.tensor_scalar_max(out=t_sbf[:], in0=t_sbf[:], scalar1=1e-8)

        ph.op("vector", f_sbf, ev="tsbf", waits=["tgsum"])

        for pc in range(16):
            w = ["tsbf"]
            if pc >= 2:
                w.append(f"tsbcev{pc - 2}")
            ph.op("tensor",
                  lambda t, pc=pc: t.matmul(p_sbc[pc % 2][:], r_s[:, pc, :],
                                            t_sbf[:], start=True, stop=True),
                  ev=f"tsbc{pc}", waits=w)
            wse = [f"tsbc{pc}"]
            if pc >= 2:
                wse.append(f"tw1_{pc - 2}")
            ph.op("scalar",
                  lambda sc, pc=pc: sc.copy(out=t_sbc[:, pc % 2, :],
                                            in_=p_sbc[pc % 2][:]),
                  ev=f"tsbcev{pc}", waits=wse)

            def f_w1(v, pc=pc, bi=bi):
                sbc = t_sbc[:, pc % 2, :]
                v.scalar_tensor_tensor(out=t_m[:, pc, :], in0=t_a[:, pc, :],
                                       scalar=ic_col[:, 0:1], in1=sbc,
                                       op0=ALU.mult, op1=ALU.is_gt)
                w1b = t_w1[bi % 2]
                v.tensor_tensor(out=w1b[:, pc, :], in0=t_sg[:, pc, :],
                                in1=sbc, op=ALU.mult)
                return v.tensor_tensor(out=w1b[:, pc, :], in0=w1b[:, pc, :],
                                       in1=t_m[:, pc, :], op=ALU.mult)

            ph.op("vector", f_w1, ev=f"tw1_{pc}",
                  waits=[f"tsbcev{pc}", "tsg"])

        ph.emit()

    ph = mk.phase("terntail")
    ph.op("gpsimd", lambda g: tern_store(g, len(tern_blocks) - 1), dma=True)
    ph.emit()

    tern.close()

    if debug == "tern":
        d1 = dbg_tensor("dbg_w1fc", [HID, HIDDEN], BF16)
        d2 = dbg_tensor("dbg_w1pr", [HIDDEN, HID], BF16)
        d3 = dbg_tensor("dbg_ta", [128, 16, 512], BF16)
        d4 = dbg_tensor("dbg_tm", [128, 16, 512], BF16)
        d5 = dbg_tensor("dbg_tsbc", [128, 2, 512], BF16)
        d6 = dbg_tensor("dbg_ic", [128, 1], F32)
        d8 = dbg_tensor("dbg_tiny", [128, 4], F32)
        d9 = dbg_tensor("dbg_tinyb", [128, 1], BF16)
        d7 = dbg_tensor("dbg_tsbf", [32, 512], BF16)
        ph = mk.phase("dumptern")
        ph.op("gpsimd", lambda g: g.dma_start(out=d3, in_=t_a[:]), dma=True)
        ph.op("gpsimd", lambda g: g.dma_start(out=d4, in_=t_m[:]), dma=True)
        ph.op("gpsimd", lambda g: g.dma_start(out=d5, in_=t_sbc[:]), dma=True)
        ph.op("gpsimd", lambda g: g.dma_start(out=d6, in_=ic_col[:]), dma=True)
        ph.op("gpsimd", lambda g: g.dma_start(out=d8, in_=tiny[:]), dma=True)
        ph.op("gpsimd", lambda g: g.dma_start(out=d9, in_=tinyb[:]), dma=True)
        ph.op("gpsimd", lambda g: g.dma_start(out=d7, in_=t_sbf[:]), dma=True)
        ph.op("gpsimd", lambda g: g.dma_start(out=d1, in_=w1fc_d), dma=True)
        ph.op("gpsimd", lambda g: g.dma_start(out=d2, in_=w1pr_d), dma=True)
        ph.op("gpsimd", lambda g: g.wait_ge(S["dma_gpsimd"].h,
                                            S["dma_gpsimd"].n) or g.nop())
        ph.emit()
        ctx.close()
        return nc, dbg

    ofm_st = ExitStack()
    o_fm = ofm_st.enter_context(nc.sbuf_tensor("o_fm", [128, 16, 1024], BF16))

    att = ExitStack()
    ae = att.enter_context
    xTbf = ae(nc.sbuf_tensor("xTbf", [128, 16, 2048], BF16))
    kfmT = ae(nc.sbuf_tensor("kfmT", [128, 4, 2048], BF16))
    qfmT = ae(nc.sbuf_tensor("qfmT", [128, 4, 1024], BF16))
    v_sb = ae(nc.sbuf_tensor("v_sb", [128, 16, 4, 132], BF16))
    s2_sb = ae(nc.sbuf_tensor("s2_sb", [128, 16, 512], BF16))
    ktok = ae(nc.sbuf_tensor("ktok", [128, 512], BF16))
    sq_scr = ae(nc.sbuf_tensor("sq_scr", [128, 2, 512], BF16))
    mask_sb = ae(nc.sbuf_tensor("mask_sb", [128, 2, 512], BF16))
    odiv = ae(nc.sbuf_tensor("odiv", [128, 4, 128], BF16))
    zc = ae(nc.sbuf_tensor("zc", [128, 8], F32))

    p_proj = [ae(nc.psum_tensor(f"p_proj{i}", [128, 512], F32)) for i in (0, 1)]
    p_tr = [ae(nc.psum_tensor(f"p_tr{i}", [128, 128], F32)) for i in (0, 1)]
    p_sc = [ae(nc.psum_tensor(f"p_sc{i}", [128, 512], F32)) for i in (0, 1)]
    p_opack = [ae(nc.psum_tensor(f"p_op{i}", [128, 320], F32)) for i in (0, 1)]
    p_o = [p_opack[0][:, 0:132], p_opack[0][:, 160:292],
           p_opack[1][:, 0:132], p_opack[1][:, 160:292]]

    xT_v = xT.rearrange("(dc p) t -> dc p t", p=128)
    ph = mk.phase("loadx")
    for dc in range(16):
        ph.op("gpsimd", lambda g, dc=dc: g.dma_start(out=xTbf[:, dc, :],
                                                     in_=xT_v[dc]),
              ev=f"xl{dc}", dma=True)
    ph.op("vector", lambda v: v.memset(v_sb[:], 1.0))
    ph.emit()

    if debug == "loadx":
        d1 = dbg_tensor("dbg_xTbf", [128, 16, 2048], BF16)
        ph = mk.phase("dumpx")
        ph.op("gpsimd", lambda g: g.dma_start(out=d1, in_=xTbf[:]), dma=True)
        ph.op("gpsimd", lambda g: g.wait_ge(S["dma_gpsimd"].h,
                                            S["dma_gpsimd"].n) or g.nop())
        ph.emit()
        att.close()
        ofm_st.close()
        ctx.close()
        return nc, dbg

    wq_v = wqT.rearrange("(dc p) f -> dc p f", p=128)
    wk_v = wkT.rearrange("(dc p) f -> dc p f", p=128)
    wv_v = wvT.rearrange("(dc p) f -> dc p f", p=128)

    sqs_hist = []
    odiv_hist = []
    otr_hist = []

    def p2a(g, wgrp):
        w_stage, w_bf = wgrp
        ph = mk.phase(f"p2a_{g}")
        specs = [("k", wk_v, True, 16), ("v", wv_v, False, 16),
                 ("q", wq_v, True, 8)]
        trcnt = [0]

        prev_ntt = {0: None, 1: 16, 2: 16}
        for wi, (nm, wv_, hasfm, ntt) in enumerate(specs):
            for qt in range(4):
                w = []
                if (wi, qt) != (0, 0):
                    pq = (wi * 4 + qt) - 1
                    w.append(f"g{g}wc{pq // 4}_{pq % 4}")
                if qt == 0 and wi >= 1:
                    w.append(f"g{g}pj{wi - 1}_{prev_ntt[wi] - 1}")
                ph.op("sync", lambda s, wv_=wv_, qt=qt: s.dma_start(
                    out=w_stage[:],
                    in_=wv_[qt * 4:(qt + 1) * 4, :, g * 512:(g + 1) * 512]
                    .rearrange("dc p f -> p dc f")),
                    ev=f"g{g}wl{wi}_{qt}", waits=w, dma=True)

                def f_cast(v, wi=wi, qt=qt, hasfm=hasfm):
                    for dcl in range(4):
                        dc = qt * 4 + dcl
                        if hasfm:
                            ins = v.tensor_tensor(out=w_bf[:, dc, :],
                                                  in0=w_stage[:, dcl, :],
                                                  in1=gbc_s[:], op=ALU.mult)
                        else:
                            ins = v.tensor_copy(out=w_bf[:, dc, :],
                                                in_=w_stage[:, dcl, :])
                    return ins

                ph.op("vector", f_cast, ev=f"g{g}wc{wi}_{qt}",
                      waits=[f"g{g}wl{wi}_{qt}"])

            for tt in range(ntt):
                w = [f"g{g}wc{wi}_3"]
                if tt >= 2:
                    w.append(f"g{g}pf{wi}_{tt - 2}")
                elif wi >= 1:
                    w.append(f"g{g}pf{wi - 1}_{prev_ntt[wi] - 2 + tt}")

                def f_proj(t, wi=wi, tt=tt, hasfm=hasfm):
                    ps = p_proj[tt % 2]
                    for dc in range(16):
                        ins = t.matmul(
                            ps[:], xTbf[:, dc, tt * 128:(tt + 1) * 128],
                            w_bf[:, dc, :], start=(dc == 0),
                            stop=(dc == 15 and not hasfm))
                    if hasfm:
                        ins = t.matmul(ps[:], onesrow[:], brow_s[:],
                                       start=False, stop=True)
                    return ins

                ph.op("tensor", f_proj, ev=f"g{g}pj{wi}_{tt}", waits=w)

                if nm == "v":
                    def f_vev(sc, tt=tt):
                        ps = p_proj[tt % 2]
                        for seg in range(4):
                            ins = sc.copy(
                                out=v_sb[:, tt, seg, 0:128],
                                in_=ps[:, seg * 128:(seg + 1) * 128])
                        return ins

                    ph.op("scalar", f_vev, ev=f"g{g}pf{wi}_{tt}",
                          waits=[f"g{g}pj{wi}_{tt}"])
                    continue

                def f_bnS(v, wi=wi, tt=tt):
                    ps = p_proj[tt % 2]
                    for seg in range(4):
                        ins = v.bn_stats(out=stat6[:, seg * 6:seg * 6 + 6],
                                         in_=ps[:, seg * 128:(seg + 1) * 128])
                    return ins

                ph.op("vector", f_bnS, ev=f"g{g}bnS{wi}_{tt}",
                      waits=[f"g{g}pj{wi}_{tt}"])

                def f_bn(v, wi=wi, tt=tt):
                    for seg in range(4):
                        ins = v.bn_aggr(
                            out=stg[:, 2 * seg:2 * seg + 2],
                            in_=stat6[:, seg * 6:seg * 6 + 6])
                    return ins

                ph.op("vector", f_bn, ev=f"g{g}bn{wi}_{tt}",
                      waits=[f"g{g}bnS{wi}_{tt}"])

                def f_sq(sc, wi=wi, tt=tt, nm=nm):
                    scl = 128.0 if nm == "q" else 1.0
                    return sc.activation(out=sm_c[:],
                                         in_=stg[:, 1:8:2], func=AF.Sqrt,
                                         scale=scl, bias=scl * EPS_LN)

                ph.op("scalar", f_sq,
                      ev=f"g{g}sq{wi}_{tt}", waits=[f"g{g}bn{wi}_{tt}"])

                ph.op("vector",
                      lambda v: v.reciprocal(out=sm_b[:], in_=sm_c[:]),
                      ev=f"g{g}rc{wi}_{tt}", waits=[f"g{g}sq{wi}_{tt}"])

                def f_fm(v, wi=wi, tt=tt, nm=nm):
                    ps = p_proj[tt % 2]
                    for seg in range(4):
                        ins = v.tensor_scalar(
                            out=ktok[:, seg * 128:(seg + 1) * 128],
                            in0=ps[:, seg * 128:(seg + 1) * 128],
                            scalar1=stg[:, 2 * seg:2 * seg + 1],
                            scalar2=sm_b[:, seg:seg + 1],
                            op0=ALU.subtract, op1=ALU.mult)
                    return ins

                wfm = [f"g{g}rc{wi}_{tt}"]
                if trcnt[0] > 0:
                    wfm.append(f"g{g}tr{trcnt[0] - 1}")
                ph.op("vector", f_fm, ev=f"g{g}pf{wi}_{tt}", waits=wfm)

                for seg in range(4):
                    i = trcnt[0]
                    trcnt[0] += 1
                    w = [f"g{g}pf{wi}_{tt}"]
                    if i >= 2:
                        w.append(f"g{g}trv{i - 2}")
                    ph.op("tensor",
                          lambda t, tt=tt, seg=seg, i=i: t.matmul(
                              p_tr[i % 2][:],
                              ktok[:, seg * 128:(seg + 1) * 128],
                              ident[:], start=True, stop=True),
                          ev=f"g{g}tr{i}", waits=w)

                    def f_trev(v, nm=nm, tt=tt, seg=seg, i=i):
                        if nm == "k":
                            dstT = kfmT[:, seg, tt * 128:(tt + 1) * 128]
                        else:
                            dstT = qfmT[:, seg, tt * 128:(tt + 1) * 128]
                        return v.tensor_copy(out=dstT, in_=p_tr[i % 2][:])

                    ph.op("vector", f_trev, ev=f"g{g}trv{i}",
                          waits=[f"g{g}tr{i}"])
        ph.emit()

    def p2b(g, h):
        hh = 4 * g + h
        ph = mk.phase(f"p2b_{hh}")
        for qg in range(2):
            for ki in range(16):
                mi = len(sqs_hist) % 2
                w = []
                if len(sqs_hist) >= 2:
                    w.append(sqs_hist[-2][1])
                ph.op("sync", lambda s, qg=qg, ki=ki, mi=mi: s.dma_start(
                    out=mask_sb[:, mi, :], in_=masks[qg * NT + ki]),
                    ev=f"h{hh}ml{qg}_{ki}", waits=w, dma=True)
                w = []
                if len(sqs_hist) >= 2:
                    w.append(sqs_hist[-2][0])
                if ki == 0 and odiv_hist:
                    w.append(odiv_hist[-1])
                ph.op("tensor", lambda t, qg=qg, ki=ki, h=h: t.matmul(
                    p_sc[len(sqs_hist) % 2 if False else (qg * 16 + ki) % 2][:],
                    kfmT[:, h, ki * 128:(ki + 1) * 128],
                    qfmT[:, h, qg * 512:(qg + 1) * 512],
                    start=True, stop=True),
                    ev=f"h{hh}sc{qg}_{ki}", waits=w)
                wsq = [f"h{hh}sc{qg}_{ki}"]
                if len(sqs_hist) >= 2:
                    wsq.append(sqs_hist[-2][1])
                ph.op("scalar", lambda sc, qg=qg, ki=ki, mi=mi: sc.activation(
                    out=sq_scr[:, mi, :], in_=p_sc[(qg * 16 + ki) % 2][:],
                    func=AF.Square),
                    ev=f"h{hh}sq{qg}_{ki}", waits=wsq)
                ph.op("vector", lambda v, qg=qg, ki=ki, mi=mi: v.tensor_tensor(
                    out=s2_sb[:, ki, :], in0=sq_scr[:, mi, :],
                    in1=mask_sb[:, mi, :], op=ALU.mult),
                    ev=f"h{hh}s2_{qg}_{ki}",
                    waits=[f"h{hh}sq{qg}_{ki}", f"h{hh}ml{qg}_{ki}"])
                sqs_hist.append((f"h{hh}sq{qg}_{ki}", f"h{hh}s2_{qg}_{ki}"))
            for qtl in range(4):
                w = [f"h{hh}s2_{qg}_15"]
                if qtl == 0 and odiv_hist:
                    w.append(odiv_hist[-1])

                def f_av(t, qg=qg, qtl=qtl, h=h):
                    po = p_o[qtl]
                    for ki in range(16):
                        ins = t.matmul(po[:, 0:129],
                                       s2_sb[:, ki, qtl * 128:(qtl + 1) * 128],
                                       v_sb[:, ki, h, 0:129],
                                       start=(ki == 0), stop=(ki == 15))
                    return ins

                ph.op("tensor", f_av, ev=f"h{hh}av{qg}_{qtl}", waits=w)

            def f_divA(v, qg=qg):
                for qtl in range(4):
                    ins = v.tensor_scalar_add(out=zc[:, qtl:qtl + 1],
                                              in0=p_o[qtl][:, 128:129],
                                              scalar1=EPS_ATTN)
                return ins

            ph.op("vector", f_divA, ev=f"h{hh}da{qg}",
                  waits=[f"h{hh}av{qg}_3"])

            def f_divB(v, qg=qg):
                for qtl in range(4):
                    ins = v.reciprocal(out=zc[:, 4 + qtl:5 + qtl],
                                       in_=zc[:, qtl:qtl + 1])
                return ins

            ph.op("vector", f_divB, ev=f"h{hh}dr{qg}",
                  waits=[f"h{hh}da{qg}"])

            def f_div(v, qg=qg):
                for qtl in range(4):
                    ins = v.tensor_scalar_mul(out=odiv[:, qtl, :],
                                              in0=p_o[qtl][:, 0:128],
                                              scalar1=zc[:, 4 + qtl:5 + qtl])
                return ins

            ph.op("vector", f_div, ev=f"h{hh}od{qg}",
                  waits=[f"h{hh}dr{qg}"])
            odiv_hist.append(f"h{hh}od{qg}")
            for qtl in range(4):
                i = len(otr_hist)
                w = [f"h{hh}od{qg}"]
                if i >= 2:
                    w.append(otr_hist[-2])
                ph.op("tensor", lambda t, qtl=qtl, i=i: t.matmul(
                    p_tr[i % 2][:], odiv[:, qtl, :], ident[:],
                    start=True, stop=True),
                    ev=f"h{hh}otr{qg}_{qtl}", waits=w)
                ph.op("vector", lambda v, hh=hh, qg=qg, qtl=qtl, i=i:
                      v.tensor_copy(
                          out=o_fm[:, hh, (qg * 4 + qtl) * 128:
                                   (qg * 4 + qtl + 1) * 128],
                          in_=p_tr[i % 2][:]),
                      ev=f"h{hh}otrv{qg}_{qtl}", waits=[f"h{hh}otr{qg}_{qtl}"])
                otr_hist.append(f"h{hh}otrv{qg}_{qtl}")
        ph.emit()

    for g in range(4):
        wgrp_st = ExitStack()
        w_stage = wgrp_st.enter_context(
            nc.sbuf_tensor(f"w_stage{g}", [128, 4, 512], F32))
        w_bf = wgrp_st.enter_context(
            nc.sbuf_tensor(f"w_bf{g}", [128, 16, 512], BF16))
        p2a(g, (w_stage, w_bf))
        wgrp_st.close()
        for h in range(4):
            p2b(g, h)

    if debug == "attn":
        d1 = dbg_tensor("dbg_ofm", [128, 16, 1024], BF16)
        d2 = dbg_tensor("dbg_kfmT", [128, 4, 2048], BF16)
        d3 = dbg_tensor("dbg_qfmT", [128, 4, 1024], BF16)
        d4 = dbg_tensor("dbg_vsb", [128, 16, 4, 132], BF16)
        d5 = dbg_tensor("dbg_s2", [128, 16, 512], BF16)
        ph = mk.phase("dumpo")
        ph.op("gpsimd", lambda gg: gg.dma_start(out=d2, in_=kfmT[:]), dma=True)
        ph.op("gpsimd", lambda gg: gg.dma_start(out=d3, in_=qfmT[:]), dma=True)
        ph.op("gpsimd", lambda gg: gg.dma_start(out=d4, in_=v_sb[:]), dma=True)
        ph.op("gpsimd", lambda gg: gg.dma_start(out=d5, in_=s2_sb[:]), dma=True)
        ph.op("gpsimd", lambda gg: gg.dma_start(out=d1, in_=o_fm[:]), dma=True)
        ph.op("gpsimd", lambda gg: gg.wait_ge(S["dma_gpsimd"].h,
                                              S["dma_gpsimd"].n) or gg.nop())
        ph.emit()
        att.close()
        ofm_st.close()
        ctx.close()
        return nc, dbg

    att.close()

    p2c = ExitStack()
    ce = p2c.enter_context
    wo_bf = ce(nc.sbuf_tensor("wo_bf", [128, 16, 2048], BF16))
    xo_sb = [ce(nc.sbuf_tensor(f"xo_sb{i}", [128, 2048], F32)) for i in (0, 1)]
    z1 = ce(nc.sbuf_tensor("z1", [128, 2048], F32))
    x1f = ce(nc.sbuf_tensor("x1f", [128, 2048], F32))
    x1bf = ce(nc.sbuf_tensor("x1bf", [128, 2048], BF16))
    x1ts = ce(nc.sbuf_tensor("x1ts", [128, 16, 128], BF16))
    p_w = [ce(nc.psum_tensor(f"p_w{i}", [128, 512], F32)) for i in range(4)]
    p_t2 = [ce(nc.psum_tensor(f"p_t2{i}", [128, 128], F32)) for i in (0, 1)]

    wo_v = woT.rearrange("(dc p) f -> dc p f", p=128)
    x1T_v = x1T_d.rearrange("(dc p) t -> dc p t", p=128)
    xo_v = xo.rearrange("(tt p) f -> tt p f", p=128)
    x1_v = x1_d.rearrange("(tt p) f -> tt p f", p=128)

    ph = mk.phase("wol")
    for dc in range(16):
        ph.op("gpsimd", lambda g, dc=dc: g.dma_start(out=wo_bf[:, dc, :],
                                                     in_=wo_v[dc]),
              ev=f"wol{dc}", dma=True)
    ph.emit()

    for tt in range(8):
        ph = mk.phase(f"p2c_{tt}")
        ph.op("sync", lambda s, tt=tt: s.dma_start(out=xo_sb[tt % 2][:],
                                                   in_=xo_v[tt]),
              ev=f"xol{tt}", dma=True)
        for dg in range(4):
            w = []
            if tt >= 1:
                w.append(f"z1ev{tt - 1}_{dg}")

            def f_wo(t, tt=tt, dg=dg):
                for fc in range(16):
                    ins = t.matmul(p_w[dg][:], o_fm[:, fc, tt * 128:(tt + 1) * 128],
                                   wo_bf[:, fc, dg * 512:(dg + 1) * 512],
                                   start=(fc == 0), stop=(fc == 15))
                return ins

            ph.op("tensor", f_wo, ev=f"wo{tt}_{dg}", waits=w)

            ph.op("vector", lambda v, tt=tt, dg=dg: v.scalar_tensor_tensor(
                out=z1[:, dg * 512:(dg + 1) * 512], in0=p_w[dg][:],
                scalar=1.0, in1=xo_sb[tt % 2][:, dg * 512:(dg + 1) * 512],
                op0=ALU.mult, op1=ALU.add),
                ev=f"z1ev{tt}_{dg}", waits=[f"wo{tt}_{dg}", f"xol{tt}"])

        def f_ln1s(v, tt=tt):
            for sg in range(4):
                ins = v.bn_stats(out=stat6[:, sg * 6:sg * 6 + 6],
                                 in_=z1[:, sg * 512:(sg + 1) * 512])
            return ins

        ph.op("vector", f_ln1s, ev=f"ln1S{tt}",
              waits=[f"z1ev{tt}_{dg}" for dg in range(4)])
        ph.op("vector", lambda v: v.bn_aggr(out=stat2[:], in_=stat6[:]),
              ev=f"ln1a{tt}", waits=[f"ln1S{tt}"])
        ph.op("scalar", lambda sc, tt=tt: sc.activation(
            out=sm_c[:, 0:1], in_=stat2[:, 1:2], func=AF.Sqrt,
            scale=1.0, bias=EPS_LN),
            ev=f"ln1s{tt}", waits=[f"ln1a{tt}"])

        ph.op("vector",
              lambda v: v.reciprocal(out=sm_b[:, 0:1], in_=sm_c[:, 0:1]),
              ev=f"ln1r{tt}", waits=[f"ln1s{tt}"])

        def f_ln1b(v, tt=tt):
            v.tensor_scalar(out=x1f[:], in0=z1[:],
                            scalar1=stat2[:, 0:1], scalar2=sm_b[:, 0:1],
                            op0=ALU.subtract, op1=ALU.mult)
            return v.tensor_copy(out=x1bf[:], in_=x1f[:])

        ph.op("vector", f_ln1b, ev=f"x1c{tt}", waits=[f"ln1r{tt}"])
        ph.op("gpsimd", lambda g, tt=tt: g.dma_start(out=x1_v[tt],
                                                     in_=x1f[:]),
              ev=f"x1sp{tt}", waits=[f"x1c{tt}"], dma=True)
        for dc in range(16):
            i = tt * 16 + dc
            w = [f"x1c{tt}"]
            if dc >= 2:
                w.append(f"x1trv{tt}_{dc - 2}")
            ph.op("tensor", lambda t, tt=tt, dc=dc: t.matmul(
                p_t2[dc % 2][:], x1bf[:, dc * 128:(dc + 1) * 128], ident[:],
                start=True, stop=True),
                ev=f"x1tr{tt}_{dc}", waits=w)
            ph.op("vector", lambda v, tt=tt, dc=dc: v.tensor_copy(
                out=x1ts[:, dc, :], in_=p_t2[dc % 2][:]),
                ev=f"x1trv{tt}_{dc}", waits=[f"x1tr{tt}_{dc}"])
        ph.op("gpsimd", lambda g, tt=tt: g.dma_start(
            out=x1T_v[:, :, tt * 128:(tt + 1) * 128]
            .rearrange("dc p t -> p dc t"),
            in_=x1ts[:]),
            ev=f"x1Tst{tt}", waits=[f"x1trv{tt}_{dc}" for dc in range(16)],
            dma=True)
        ph.emit()

    p2c.close()
    ofm_st.close()

    if debug == "ln1":
        d1 = dbg_tensor("dbg_x1", [OWN * 128, HID], F32)
        ph = mk.phase("dumpx1")
        ph.op("gpsimd", lambda g: g.dma_start(out=d1, in_=x1_d), dma=True)
        ph.op("gpsimd", lambda g: g.wait_ge(S["dma_gpsimd"].h,
                                            S["dma_gpsimd"].n) or g.nop())
        ph.emit()
        ctx.close()
        return nc, dbg

    p3 = ExitStack()
    fe = p3.enter_context
    h_act = fe(nc.sbuf_tensor("h_act", [128, 64, 512], BF16))
    x1Tsb = fe(nc.sbuf_tensor("x1Tsb", [128, 16, 512], BF16))
    wblk = [fe(nc.sbuf_tensor(f"wblk{i}", [128, 16, 512], BF16))
            for i in (0, 1)]
    hr_scr = fe(nc.sbuf_tensor("hr_scr", [128, 2, 512], F32))
    hsq_scr = [fe(nc.sbuf_tensor(f"hsq{i}", [128, 512], BF16)) for i in (0, 1)]
    x1c_sb = [fe(nc.sbuf_tensor(f"x1cs{i}", [128, 512], F32)) for i in (0, 1)]
    z2 = fe(nc.sbuf_tensor("z2", [128, 4, 2048], F32))
    outt = fe(nc.sbuf_tensor("outt", [128, 2048], F32))
    ms_row = fe(nc.sbuf_tensor("ms_row", [1, 512], F32))
    p_h = [fe(nc.psum_tensor(f"p_h{i}", [128, 512], F32)) for i in (0, 1)]
    p_ms = fe(nc.psum_tensor("p_ms", [1, 512], F32))
    p_mini = fe(nc.psum_tensor("p_mini", [128, 4], F32))
    p_y = [fe(nc.psum_tensor(f"p_y{i}", [128, 512], F32)) for i in range(4)]

    w1fc_v = w1fc_d.rearrange("(dc p) f -> dc p f", p=128)
    w1pr_v = w1pr_d.rearrange("(rb hc p) f -> rb hc p f", p=128, hc=16)
    out_v = out.rearrange("(tt p) f -> tt p f", p=128)
    x1_v3 = x1_d.rearrange("(tt p) f -> tt p f", p=128)

    for tg in range(2):
        ph = mk.phase(f"p3a_{tg}")
        ph.op("sync", lambda s, tg=tg: s.dma_start(
            out=x1Tsb[:],
            in_=x1T_v[:, :, tg * 512:(tg + 1) * 512]
            .rearrange("dc p t -> p dc t")),
            ev=f"A{tg}x1T", dma=True)
        for blkq in range(16):
            w = [f"A{tg}re{blkq - 1}_3"] if blkq >= 1 else []
            ph.op("sync", lambda s, blkq=blkq: s.dma_start(
                out=wblk[blkq % 2][:],
                in_=w1fc_v[:, :, blkq * 512:(blkq + 1) * 512]
                .rearrange("dc p f -> p dc f")),
                ev=f"A{tg}wl{blkq}", waits=w, dma=True)
            for htl in range(4):
                ht = blkq * 4 + htl
                w = [f"A{tg}wl{blkq}"]
                if ht >= 2:
                    pht = ht - 2
                    w.append(f"A{tg}re{pht // 4}_{pht % 4}")

                def f_h(t, blkq=blkq, htl=htl, ht=ht, tg=tg):
                    ps = p_h[ht % 2]
                    for dc in range(16):
                        ins = t.matmul(
                            ps[:], wblk[blkq % 2][:, dc,
                                                  htl * 128:(htl + 1) * 128],
                            x1Tsb[:, dc, :],
                            start=(dc == 0), stop=(dc == 15))
                    return ins

                w.append(f"A{tg}x1T")
                ph.op("tensor", f_h, ev=f"A{tg}hp{blkq}_{htl}", waits=w)
                w = [f"A{tg}hp{blkq}_{htl}"]
                if ht >= 2:
                    pht = ht - 2
                    w.append(f"A{tg}ha{pht // 4}_{pht % 4}")
                ph.op("scalar", lambda sc, ht=ht: sc.activation(
                    out=hr_scr[:, ht % 2, :], in_=p_h[ht % 2][:],
                    func=AF.Relu),
                    ev=f"A{tg}re{blkq}_{htl}", waits=w)

                def f_ha(v, ht=ht, blkq=blkq, htl=htl):
                    hr = hr_scr[:, ht % 2, :]
                    v.tensor_tensor(out=h_act[:, ht, :], in0=hr,
                                    in1=hr, op=ALU.mult)
                    return v.tensor_tensor(out=hsq_scr[ht % 2][:],
                                           in0=h_act[:, ht, :],
                                           in1=h_act[:, ht, :], op=ALU.mult)

                w = [f"A{tg}re{blkq}_{htl}"]
                if ht >= 2:
                    w.append(f"A{tg}zm{ht - 2}")
                ph.op("vector", f_ha, ev=f"A{tg}ha{blkq}_{htl}", waits=w)
                ph.op("tensor", lambda t, ht=ht: t.matmul(
                    p_ms[:], onescol[:], hsq_scr[ht % 2][:],
                    start=(ht == 0), stop=(ht == 63)),
                    ev=f"A{tg}zm{ht}", waits=[f"A{tg}ha{blkq}_{htl}"])
        ph.op("vector", lambda v: v.tensor_copy(out=ms_row[:], in_=p_ms[:]),
              ev=f"A{tg}msr", waits=[f"A{tg}zm63"])
        for j in range(4):
            ph.op("tensor", lambda t, j=j: t.matmul(
                p_mini[:, j:j + 1], ms_row[0:1, j * 128:(j + 1) * 128],
                onesf[:], start=True, stop=True),
                ev=f"A{tg}mt{j}", waits=[f"A{tg}msr"])

        ph.op("scalar", lambda sc: sc.activation(
            out=sm_c[:], in_=p_mini[:], func=AF.Sqrt,
            scale=1.0 / HIDDEN, bias=EPS_RMS),
            ev=f"A{tg}rms", waits=[f"A{tg}mt{j}" for j in range(4)])

        ph.op("vector", lambda v, tg=tg: v.reciprocal(
            out=rstd_cols[:, tg * 4:(tg + 1) * 4], in_=sm_c[:]),
            ev=f"A{tg}rmb", waits=[f"A{tg}rms"])
        ph.emit()

        ph = mk.phase(f"p3b_{tg}")
        nload = [0]
        for dq in range(4):
            for hmc in range(4):
                w = []
                if dq * 4 + hmc >= 2:
                    pi = dq * 4 + hmc - 2
                    w.append(f"B{tg}ym{pi // 4}_{pi % 4}")
                ph.op("sync", lambda s, dq=dq, hmc=hmc: s.dma_start(
                    out=wblk[hmc % 2][:],
                    in_=w1pr_v[hmc, :, :, dq * 512:(dq + 1) * 512]
                    .rearrange("hc p f -> p hc f")),
                    ev=f"B{tg}wl{dq}_{hmc}", waits=w, dma=True)
                w = [f"B{tg}wl{dq}_{hmc}"]
                if hmc == 0 and dq >= 1:
                    w += [f"B{tg}z2_{dq - 1}_{ttl}" for ttl in range(4)]

                def f_y(t, dq=dq, hmc=hmc, tg=tg):
                    for hcl in range(16):
                        hc = hmc * 16 + hcl
                        for ttl in range(4):
                            ins = t.matmul(
                                p_y[ttl][:],
                                h_act[:, hc, ttl * 128:(ttl + 1) * 128],
                                wblk[hmc % 2][:, hcl, :],
                                start=(hc == 0), stop=(hc == 63))
                    return ins

                ph.op("tensor", f_y, ev=f"B{tg}ym{dq}_{hmc}", waits=w)
            for ttl in range(4):
                li = nload[0]
                nload[0] += 1
                w = []
                if li >= 2:
                    lj = li - 2
                    w.append(f"B{tg}z2_{lj // 4}_{lj % 4}")
                ph.op("sync", lambda s, dq=dq, ttl=ttl, li=li, tg=tg:
                      s.dma_start(out=x1c_sb[li % 2][:],
                                  in_=x1_v3[tg * 4 + ttl, :,
                                            dq * 512:(dq + 1) * 512]),
                      ev=f"B{tg}xl{dq}_{ttl}", waits=w, dma=True)
                ph.op("vector", lambda v, dq=dq, ttl=ttl, li=li, tg=tg:
                      v.scalar_tensor_tensor(
                          out=z2[:, ttl, dq * 512:(dq + 1) * 512],
                          in0=p_y[ttl][:],
                          scalar=rstd_cols[:, tg * 4 + ttl:tg * 4 + ttl + 1],
                          in1=x1c_sb[li % 2][:],
                          op0=ALU.mult, op1=ALU.add),
                      ev=f"B{tg}z2_{dq}_{ttl}",
                      waits=[f"B{tg}ym{dq}_3", f"B{tg}xl{dq}_{ttl}"])
        for ttl in range(4):
            def f_ln2s(v, ttl=ttl):
                for sg in range(4):
                    ins = v.bn_stats(out=stat6[:, sg * 6:sg * 6 + 6],
                                     in_=z2[:, ttl, sg * 512:(sg + 1) * 512])
                return ins

            ph.op("vector", f_ln2s, ev=f"B{tg}lnS{ttl}",
                  waits=[f"B{tg}z2_3_{ttl}"])
            ph.op("vector", lambda v: v.bn_aggr(out=stat2[:], in_=stat6[:]),
                  ev=f"B{tg}lna{ttl}", waits=[f"B{tg}lnS{ttl}"])
            ph.op("scalar", lambda sc: sc.activation(
                out=sm_c[:, 0:1], in_=stat2[:, 1:2], func=AF.Sqrt,
                scale=1.0, bias=EPS_LN),
                ev=f"B{tg}lns{ttl}", waits=[f"B{tg}lna{ttl}"])

            ph.op("vector",
                  lambda v: v.reciprocal(out=sm_b[:, 0:1], in_=sm_c[:, 0:1]),
                  ev=f"B{tg}lnr{ttl}", waits=[f"B{tg}lns{ttl}"])

            def f_ln2b(v, ttl=ttl):
                return v.tensor_scalar(out=outt[:], in0=z2[:, ttl, :],
                                       scalar1=stat2[:, 0:1],
                                       scalar2=sm_b[:, 0:1],
                                       op0=ALU.subtract, op1=ALU.mult)

            wl2 = [f"B{tg}lnr{ttl}"]
            if ttl >= 1:
                wl2.append(f"B{tg}st{ttl - 1}")
            ph.op("vector", f_ln2b, ev=f"B{tg}lnb{ttl}", waits=wl2)
            ph.op("gpsimd", lambda g, ttl=ttl, tg=tg: g.dma_start(
                out=out_v[tg * 4 + ttl], in_=outt[:]),
                ev=f"B{tg}st{ttl}", waits=[f"B{tg}lnb{ttl}"], dma=True)
            if ttl < 3:
                pass
        ph.emit()

    if debug == "ffn":
        d1 = dbg_tensor("dbg_z2", [128, 4, 2048], F32)
        d2 = dbg_tensor("dbg_hact", [128, 64, 512], BF16)
        d3 = dbg_tensor("dbg_rstd", [128, 8], F32)
        ph = mk.phase("dumpffn")
        ph.op("gpsimd", lambda g: g.dma_start(out=d1, in_=z2[:]), dma=True)
        ph.op("gpsimd", lambda g: g.dma_start(out=d2, in_=h_act[:]), dma=True)
        ph.op("gpsimd", lambda g: g.dma_start(out=d3, in_=rstd_cols[:]),
              dma=True)
        ph.emit()

    p3.close()

    ph = mk.phase("fin")
    ph.op("gpsimd", lambda g: g.wait_ge(S["dma_gpsimd"].h, S["dma_gpsimd"].n))
    ph.emit()

    ctx.close()
    return nc, dbg


def _stage_inputs(inputs):
    x = np.asarray(inputs["L0_attn_hidden_states"])
    Wq = np.asarray(inputs["L0_attn_Wq"])
    Wk = np.asarray(inputs["L0_attn_Wk"])
    Wv = np.asarray(inputs["L0_attn_Wv"])
    Wo = np.asarray(inputs["L0_attn_Wo"])
    gamma = np.asarray(inputs["L0_attn_gamma"])
    beta = np.asarray(inputs["L0_attn_beta"])
    Wfc = np.asarray(inputs["L0_ffn_W_fc"])
    Wpr = np.asarray(inputs["L0_ffn_W_proj"])

    wqT = np.ascontiguousarray(Wq.T)
    wkT = np.ascontiguousarray(Wk.T)
    wvT = np.ascontiguousarray(Wv.T)
    woT = np.ascontiguousarray(Wo.T)
    wfcT = np.ascontiguousarray(Wfc.T)
    wprT = np.ascontiguousarray(Wpr.T)
    gbc = np.ascontiguousarray(
        np.broadcast_to(np.tile(gamma, 4)[None, :], (128, 512))).astype(
            np.float32, copy=False)
    brow = np.ascontiguousarray(np.tile(beta, 4)[None, :]).astype(
        np.float32, copy=False)

    in_maps = []
    perms = []
    for c in range(N_CORES):
        b = c // 2
        perm = perm_tiles_for(c)
        perms.append(perm)
        cols = np.concatenate([np.arange(t * 128, (t + 1) * 128)
                               for t in perm])
        own_rows = cols[:OWN * 128]
        xT_c = np.ascontiguousarray(x[b].T[:, cols])
        xo_c = np.ascontiguousarray(x[b][own_rows])
        in_maps.append({
            "xT": xT_c, "xo": xo_c,
            "wqT": wqT, "wkT": wkT, "wvT": wvT, "woT": woT,
            "wfcT": wfcT, "wprT": wprT,
            "gbc": gbc, "brow": brow,
            "masks": np.ascontiguousarray(build_masks(c)),
        })
    return in_maps, perms


_CACHED = {}


def kernel(**inputs):
    if "nc" not in _CACHED:
        nc, _ = build_program()
        _CACHED["nc"] = nc
    nc = _CACHED["nc"]
    in_maps, perms = _stage_inputs(inputs)
    res = run_bass_kernel_spmd(nc, in_maps, core_ids=list(range(N_CORES)))
    outp = np.zeros((B, T, HID), dtype=np.float32)
    for c in range(N_CORES):
        b = c // 2
        own = perms[c][:OWN]
        r = np.asarray(res.results[c]["out"])
        for i, t in enumerate(own):
            outp[b, t * 128:(t + 1) * 128] = r[i * 128:(i + 1) * 128]
    return outp
